# version 34 (final)
# baseline (speedup 1.0000x reference)
import numpy as np
from contextlib import ExitStack

import concourse.bass as bass
import concourse.mybir as mybir
from concourse.bass_utils import run_bass_kernel_spmd

F32 = mybir.dt.float32
BF16 = mybir.dt.bfloat16
U8 = mybir.dt.uint8
AF = mybir.ActivationFunctionType
ALU = mybir.AluOpType
AX = mybir.AxisListType

NCORE = 8
S = 2048
D = 2048
II = 8192
FC = 16
OC = 16
SC = 16
SG = 4
ICN = 64
HCN = 16
LN_EPS = 1e-5

PP_BQK, PP_BU, PP_LNG, PP_LNB, PP_GS, PP_US, PP_DS = 0, 16, 32, 48, 64, 128, 192
PP_COLS = 208

ENGS = ("tensor", "scalar", "vector", "gpsimd", "sync")
ENG_SEM = {"tensor": "pe", "scalar": "act", "vector": "dve", "gpsimd": "pool", "sync": "syn"}


class Buf:
    __slots__ = ("w", "readers")

    def __init__(self):
        self.w = None
        self.readers = []


class Sched:

    def __init__(self, nc, es_sem):
        self.nc = nc
        self.es_sem = es_sem
        self.semh = {}
        self.cnt = {}
        self.known = {e: {} for e in ENGS}
        self.ops = {e: [] for e in ENGS}
        self.pend = {e: [] for e in ENGS}
        self.nops = 0
        for e in ENGS:
            self.sem(ENG_SEM[e])

    def sem(self, name):
        if name not in self.semh:
            self.semh[name] = self.es_sem.enter_context(self.nc.semaphore(name))
            self.cnt[name] = 0
        return name

    def emit(self, eng, fn, reads=(), writes=(), waits=(), sig=True, sem=None,
             by=1, known=True, is_dma=False):
        evs = []
        for bf in reads:
            if bf.w:
                evs.append(bf.w)
        for bf in writes:
            if bf.w:
                evs.append(bf.w)
            evs.extend(bf.readers)
        evs.extend(w for w in waits if w)
        kn = self.known[eng]
        need = {}
        for sname, val in evs:
            if val > 0 and need.get(sname, 0) < val:
                need[sname] = val
        wl = []
        own = ENG_SEM[eng]
        for sname, val in need.items():
            if kn.get(sname, 0) < val or (is_dma and sname == own):
                wl.append((sname, val))
                kn[sname] = val
        ev = None
        if sig:
            sname = sem if sem else ENG_SEM[eng]
            self.sem(sname)
            self.cnt[sname] += by
            ev = (sname, self.cnt[sname])
            if known:
                kn[sname] = self.cnt[sname]
        self.ops[eng].append((wl, fn, None if ev is None else ev[0], by))
        self.nops += 1
        if ev is not None:
            for bf in reads:
                bf.readers.append(ev)
            for bf in writes:
                bf.w = ev
                bf.readers = []
            for bf, isw in self.pend[eng]:
                if isw:
                    bf.w = ev
                    bf.readers = []
                else:
                    bf.readers.append(ev)
            self.pend[eng] = []
        else:
            for bf in reads:
                self.pend[eng].append((bf, False))
            for bf in writes:
                self.pend[eng].append((bf, True))
        return ev

    def dma(self, eng, fn, reads=(), writes=(), waits=(), sem="dma"):
        return self.emit(eng, fn, reads=reads, writes=writes, waits=waits,
                         sig=True, sem=sem, by=16, known=False, is_dma=True)

    def barrier_evs(self):
        return [(s, c) for s, c in self.cnt.items() if c > 0]

    def barrier(self):
        evs = self.barrier_evs()
        for e in ENGS:
            self.emit(e, lambda q: q.nop(), waits=evs, sig=False)

    def flush(self, blk):
        for eng in ENGS:
            ops = self.ops[eng]
            if not ops:
                continue
            self.ops[eng] = []
            sec = getattr(blk, eng)

            def body(q, ops=ops):
                for wl, fn, sname, by in ops:
                    for wn, wv in wl:
                        q.wait_ge(self.semh[wn], wv)
                    ins = fn(q)
                    if sname is not None:
                        ins.then_inc(self.semh[sname], by)
            sec(body)


class NS:
    pass


def _blk3(dst_ap, src2d, p=128):
    c = src2d.shape[0] // p
    return dst_ap.rearrange("p (c w) -> p c w", c=c), src2d.rearrange("(c p) w -> p c w", p=p)




def build(stage=3, debug=False):
    nc = bass.Bass("TRN2", target_bir_lowering=False)

    x_in = nc.dram_tensor("x", [S, D], F32, kind="ExternalInput").ap()
    wqk_in = nc.dram_tensor("wqk", [OC, 128, D], F32, kind="ExternalInput").ap()
    wv_in = nc.dram_tensor("wv", [4, 128, II], F32, kind="ExternalInput").ap()
    wu_in = nc.dram_tensor("wu", [OC, 128, D], F32, kind="ExternalInput").ap()
    wo_in = nc.dram_tensor("wo", [D, D], F32, kind="ExternalInput").ap()
    pp_in = nc.dram_tensor("pp", [128, PP_COLS], F32, kind="ExternalInput").ap()
    brow_in = nc.dram_tensor("brow", [2, D], F32, kind="ExternalInput").ap()
    guw_in = nc.dram_tensor("guw", [ICN, 128, 4096], U8, kind="ExternalInput").ap()
    wd_in = nc.dram_tensor("wd", [HCN, 128, II], U8, kind="ExternalInput").ap()
    out_t = nc.dram_tensor("out", [S, D], F32, kind="ExternalOutput").ap()

    q_scr = [nc.dram_tensor(f"q_scr{i}", [128, S], BF16) for i in range(OC)]
    u_scr = [nc.dram_tensor(f"u_scr{i}", [128, S], F32) for i in range(OC)]
    af_scr = nc.dram_tensor("af_scr", [S, D], F32)
    kv_part = nc.dram_tensor("kv_part", [128, D], F32)
    kv_sum = nc.dram_tensor("kv_sum", [128, D], F32)

    dbg = {}
    if debug:
        dbg["kv"] = nc.dram_tensor("dbg_kv", [128, D], F32, kind="ExternalOutput").ap()
        dbg["xn0"] = nc.dram_tensor("dbg_xn0", [128, S], BF16, kind="ExternalOutput").ap()
        dbg["q0"] = nc.dram_tensor("dbg_q0", [128, S], BF16, kind="ExternalOutput").ap()
        dbg["u0"] = nc.dram_tensor("dbg_u0", [128, S], F32, kind="ExternalOutput").ap()
        dbg["af"] = nc.dram_tensor("dbg_af", [S, D], F32, kind="ExternalOutput").ap()

    with ExitStack() as es0:
        P = Sched(nc, es0)
        b = NS()
        sb = lambda es, name, shape, dt: es.enter_context(nc.sbuf_tensor(name, shape, dt))
        _psn = [0]
        def psf(es, shape=(128, 512), dt=F32):
            _psn[0] += 1
            return es.enter_context(nc.psum_tensor(f'ps{_psn[0]}', list(shape), dt))

        b.xn = [sb(es0, f"xn{i}", [128, S], BF16) for i in range(FC)]
        b.p2 = [sb(es0, f"p2_{i}", [128, S], BF16) for i in range(16)]
        b.pp = sb(es0, "pp_sb", [128, PP_COLS], F32)
        b.ident_bf = sb(es0, "ident_bf", [128, 128], BF16)
        b.ones_col = sb(es0, "ones_col", [128, 1], F32)
        b.ones_col_bf = sb(es0, "ones_col_bf", [128, 1], BF16)
        b.ones_row = sb(es0, "ones_row", [1, 128], F32)
        b.eps_col = sb(es0, "eps_col", [128, 1], F32)
        b.eps_row = sb(es0, "eps_row", [1, 1], F32)
        b.stat = sb(es0, "stat", [128, 16], F32)
        b.nbqk = sb(es0, "nbqk", [128, 16], F32)

        t = NS()
        t.xn = [[Buf() for _ in range(4)] for _ in range(FC)]
        t.p2 = [Buf() for _ in range(16)]
        t.pp = Buf()
        t.setup = Buf()
        t.stat = [Buf(), Buf()]

        blk = es0.enter_context(nc.Block())

        P.emit("gpsimd", lambda q: q.memset(b.ident_bf[:], 0.0), writes=[t.setup])
        P.emit("gpsimd", lambda q: q.affine_select(
            out=b.ident_bf[:], in_=b.ident_bf[:], compare_op=ALU.not_equal,
            fill=1.0, base=0, pattern=[[-1, 128]], channel_multiplier=1), writes=[t.setup])
        P.emit("gpsimd", lambda q: q.memset(b.ones_col[:], 1.0), writes=[t.setup])
        P.emit("gpsimd", lambda q: q.memset(b.ones_col_bf[:], 1.0), writes=[t.setup])
        P.emit("gpsimd", lambda q: q.memset(b.ones_row[:], 1.0), writes=[t.setup])
        P.emit("gpsimd", lambda q: q.memset(b.eps_col[:], LN_EPS), writes=[t.setup])
        P.emit("gpsimd", lambda q: q.memset(b.eps_row[:], LN_EPS), writes=[t.setup])
        P.dma("sync", lambda q: q.dma_start(out=b.pp[:], in_=pp_in), writes=[t.pp], sem="Lpp")
        t.nbqk = Buf()
        P.emit("vector", lambda q: q.tensor_scalar(
            b.nbqk[:], b.pp[:, PP_BQK:PP_BQK + 16], -1.0, None, ALU.mult),
            reads=[t.pp], writes=[t.nbqk])

        with ExitStack() as es1:
            b.brep = sb(es1, "brep", [128, D], BF16)
            b.brow = sb(es1, "brow_v", [1, 512], F32)
            b.kv_bf = sb(es1, "kv_bf", [128, D], BF16)

            t.brep = [Buf() for _ in range(4)]
            t.brow = Buf()
            t.brow2 = Buf()
            t.kv = [Buf() for _ in range(4)]
            t.kvbf = Buf()

            with ExitStack() as esA01:
                _phase_A0(P, nc, blk, b, t, x_in, sb, psf, esA01)
                if debug:
                    P.dma("sync", lambda q: q.dma_start(out=dbg["xn0"], in_=b.xn[0][:]),
                          reads=t.xn[0], sem="Sdbg")
                _phase_A1(P, nc, blk, b, t, wqk_in, q_scr, sb, psf, esA01)
            P.barrier()
            if debug:
                P.dma("sync", lambda q: q.dma_start(out=dbg["q0"], in_=q_scr[0].ap()),
                      sem="Sdbg")
            with ExitStack() as esKV:
                b.kv_sb = sb(esKV, "kv_sb", [128, D], F32)
                b.brow2 = sb(esKV, "brow_o", [1, D], F32)
                P.emit("gpsimd", lambda q: q.memset(b.kv_sb[:], 0.0),
                       writes=[t.kv[0], t.kv[1], t.kv[2], t.kv[3]])
                _phase_A2(P, nc, blk, b, t, wv_in, brow_in, sb, psf)
                P.barrier()
                _phase_AB(P, nc, blk, b, t, wu_in, wo_in, u_scr, kv_part, kv_sum, brow_in, dbg, sb, psf)
                P.barrier()
                P.flush(blk)
            if debug:
                P.dma("sync", lambda q: q.dma_start(out=dbg["u0"], in_=u_scr[0].ap()),
                      sem="Sdbg")
                P.flush(blk)
            if stage >= 2:
                _phase_B(P, nc, blk, b, t, q_scr, u_scr, af_scr, brow_in, sb, psf)
                P.barrier()
                if debug:
                    P.dma("sync", lambda q: q.dma_start(out=dbg["af"], in_=af_scr.ap()),
                          sem="Sdbg")
                    P.flush(blk)
            P.flush(blk)
        if stage >= 3:
            _phase_F(P, nc, blk, b, t, x_in, guw_in, wd_in, af_scr, out_t, sb, psf)
        else:
            P.dma("sync", lambda q: q.dma_start(out=out_t[0:128, 0:PP_COLS], in_=b.pp[:]),
                  reads=[t.pp], waits=P.barrier_evs(), sem="Sfin")
            fin = [("Sfin", 16)]
            if "Sdbg" in P.cnt:
                fin.append(("Sdbg", P.cnt["Sdbg"]))
            P.emit("sync", lambda q: q.nop(), waits=fin, sig=False)
        P.flush(blk)

    return nc


def _phase_A0(P, nc, blk, b, t, x_in, sb, psf, es_sh):
    if True:
        es = es_sh
        xrow = [sb(es, f"xrow{i}", [128, D], F32) for i in range(2)]
        xsq = [sb(es, f"xsq{i}", [128, D], F32) for i in range(2)]
        xnr = [sb(es, f"xnr{i}", [128, D], BF16) for i in range(2)]
        ps_tr = [psf(es, (128, 1024), BF16) for _ in range(4)]
        t_xrow = [Buf(), Buf(), Buf()]
        t_xsq = [Buf(), Buf()]
        t_xnr = [Buf(), Buf()]
        t_xnrh = [Buf(), Buf()]
        t_pstr = [Buf(), Buf(), Buf(), Buf()]

        def spread(i):
            sp = i % 2
            for fcx in range(FC):
                tp = (i * FC + fcx) % 4
                th = t_xnr[sp] if fcx < 8 else t_xnrh[sp]
                P.emit("tensor", lambda q, sp=sp, fcx=fcx, tp=tp: q.transpose(
                    ps_tr[tp][:, 0:128], xnr[sp][:, fcx * 128:(fcx + 1) * 128], b.ident_bf[:]),
                    reads=[th, t.setup], writes=[t_pstr[tp]])
                P.emit("scalar", lambda q, i=i, fcx=fcx, tp=tp: q.copy(
                    b.xn[fcx][:, i * 128:(i + 1) * 128], ps_tr[tp][:, 0:128]),
                    reads=[t_pstr[tp]], writes=[t.xn[fcx][i // 4]])

        for i in range(SC):
            p = i % 2
            sp = i % 2
            P.dma("sync", lambda q, p=p, i=i: q.dma_start(
                out=xrow[p][:], in_=x_in[i * 128:(i + 1) * 128, :]),
                writes=[t_xrow[p]], sem=f"Lx{p}")
            o = 8 * sp
            C = lambda j: b.stat[:, o + j:o + j + 1]
            P.emit("scalar", lambda q, p=p, sp=sp: q.activation(xsq[sp][:], xrow[p][:], AF.Square),
                   reads=[t_xrow[p]], writes=[t_xsq[sp]])
            P.emit("vector", lambda q, p=p, C=C: q.reduce_sum(C(0), xrow[p][:], axis=AX.X),
                   reads=[t_xrow[p]], writes=[t.stat[sp]], sig=False)
            P.emit("vector", lambda q, sp=sp, C=C: q.reduce_sum(C(1), xsq[sp][:], axis=AX.X),
                   reads=[t_xsq[sp]], writes=[t.stat[sp]], sig=False)
            if i >= 1:
                spread(i - 1)
            P.emit("vector", lambda q, C=C: q.tensor_scalar(C(2), C(0), 1.0 / D, None, ALU.mult), sig=False)
            P.emit("vector", lambda q, C=C: q.tensor_scalar(C(3), C(1), 1.0 / D, None, ALU.mult),
                   writes=[t.stat[sp]])
            P.emit("scalar", lambda q, C=C: q.activation(C(4), C(2), AF.Square),
                   reads=[t.stat[sp]], writes=[t.stat[sp]])
            P.emit("vector", lambda q, C=C: q.tensor_tensor(C(5), C(3), C(4), ALU.subtract),
                   reads=[t.stat[sp]], writes=[t.stat[sp]])
            P.emit("scalar", lambda q, C=C: q.activation(C(6), C(5), AF.Ln, bias=b.eps_col[:]),
                   reads=[t.stat[sp], t.setup], sig=False)
            P.emit("scalar", lambda q, C=C: q.activation(C(4), C(6), AF.Exp, scale=-0.5),
                   writes=[t.stat[sp]])
            P.emit("vector", lambda q, C=C: q.tensor_scalar(C(7), C(2), C(4), -1.0, ALU.mult, ALU.mult),
                   reads=[t.stat[sp]], writes=[t.stat[sp]])
            P.emit("vector", lambda q, p=p, sp=sp, C=C: q.tensor_scalar(
                xnr[sp][:, 0:1024], xrow[p][:, 0:1024], C(4), C(7), ALU.mult, ALU.add),
                reads=[t_xrow[p], t.stat[sp]], writes=[t_xnr[sp]])
            P.emit("gpsimd", lambda q, p=p, sp=sp, C=C: q.tensor_scalar(
                xnr[sp][:, 1024:2048], xrow[p][:, 1024:2048], C(4), C(7), ALU.mult, ALU.add),
                reads=[t_xrow[p], t.stat[sp]], writes=[t_xnrh[sp]])
        spread(SC - 1)
        P.flush(blk)


def _phase_A1(P, nc, blk, b, t, wqk_in, q_scr, sb, psf, es_sh):
    if True:
        es = es_sh
        wq = [sb(es, f"wqA{i}", [128, D], BF16) for i in range(2)]
        qstage = [sb(es, f"qstage{i}", [128, S], BF16) for i in range(2)]
        fm = [sb(es, f"fmA{i}", [128, 512], F32) for i in range(2)]
        fp = [sb(es, f"fpA{i}", [128, 512], F32) for i in range(2)]
        ps_q = [psf(es) for _ in range(2)]
        ps_tr = [psf(es, (128, 1024), BF16) for _ in range(2)]
        t_wq = [Buf(), Buf()]
        t_qst = [Buf(), Buf()]
        t_fm = [Buf(), Buf()]
        t_fp = [Buf(), Buf()]
        t_psq = [Buf(), Buf()]
        t_pstr = [Buf(), Buf()]

        def transposes(oc, grp=None):
            rng = range(SC) if grp is None else range(grp * 4, grp * 4 + 4)
            for scx in rng:
                tp = (oc * SC + scx) % 2
                P.emit("tensor", lambda q, oc=oc, scx=scx, tp=tp: q.transpose(
                    ps_tr[tp][:, 0:128], qstage[oc % 2][:, scx * 128:(scx + 1) * 128],
                    b.ident_bf[:]),
                    reads=[t_qst[oc % 2]], writes=[t_pstr[tp]])
                P.emit("vector", lambda q, oc=oc, scx=scx, tp=tp: q.tensor_copy(
                    b.p2[oc][:, scx * 128:(scx + 1) * 128], ps_tr[tp][:, 0:128]),
                    reads=[t_pstr[tp]], writes=[t.p2[oc]])

        for oc in range(OC):
            wp = oc % 2
            P.dma("gpsimd", lambda q, oc=oc, wp=wp: q.dma_start(
                out=wq[wp][:], in_=wqk_in[oc]),
                writes=[t_wq[wp]], sem=f"Lwq{wp}")
            for sg in range(SG):
                qp = (oc * SG + sg) % 2
                for fcx in range(FC):
                    P.emit("tensor", lambda q, wp=wp, fcx=fcx, sg=sg, qp=qp: q.matmul(
                        ps_q[qp][:], wq[wp][:, fcx * 128:(fcx + 1) * 128],
                        b.xn[fcx][:, sg * 512:(sg + 1) * 512],
                        start=(fcx == 0), stop=(fcx == FC - 1)),
                        reads=[t_wq[wp], t.xn[fcx][sg]], writes=[t_psq[qp]],
                        sig=(fcx == FC - 1))
                if oc >= 1:
                    transposes(oc - 1, sg)
                bias = b.pp[:, PP_BQK + oc:PP_BQK + oc + 1]
                nbias = b.nbqk[:, oc:oc + 1]
                P.emit("scalar", lambda q, qp=qp, nbias=nbias: q.activation(
                    fm[qp][:], ps_q[qp][:], AF.Relu, bias=nbias, scale=-1.0),
                    reads=[t_psq[qp], t.nbqk], writes=[t_fm[qp]])
                P.emit("scalar", lambda q, qp=qp, bias=bias: q.activation(
                    fp[qp][:], ps_q[qp][:], AF.Relu, bias=bias),
                    reads=[t_psq[qp], t.pp], writes=[t_fp[qp]])
                P.emit("scalar", lambda q, qp=qp: q.activation(fm[qp][:], fm[qp][:], AF.Exp, scale=-1.0),
                       writes=[t_fm[qp]])
                P.emit("vector", lambda q, oc=oc, sg=sg, qp=qp: q.scalar_tensor_tensor(
                    qstage[oc % 2][:, sg * 512:(sg + 1) * 512],
                    fm[qp][:], -1.0, fp[qp][:], ALU.add, ALU.add),
                    reads=[t_fm[qp], t_fp[qp]], writes=[t_qst[oc % 2]])
            P.dma("scalar", lambda q, oc=oc: q.dma_start(
                out=q_scr[oc].ap(), in_=qstage[oc % 2][:]),
                reads=[t_qst[oc % 2]], sem=f"Sq{oc % 2}")
        transposes(OC - 1)
        P.flush(blk)


def _phase_A2(P, nc, blk, b, t, wv_in, brow_in, sb, psf):
    with ExitStack() as es:
        wvb = [sb(es, "wvb0", [128, II], BF16), sb(es, "wvb1", [128, II], BF16)]
        vt = [sb(es, f"vt{i}", [128, 512], BF16) for i in range(2)]
        fv = [sb(es, f"fv{i}", [128, 512], F32) for i in range(2)]
        ps_v = [psf(es) for _ in range(2)]
        ps_kv = [psf(es) for _ in range(2)]
        ps_rep = psf(es)
        t_wv = [Buf(), Buf()]
        t_vt = [Buf(), Buf()]
        t_fv = [Buf(), Buf()]
        t_psv = [Buf(), Buf()]
        t_pskv = [Buf(), Buf()]
        t_psrep = Buf()

        def kv_part_mms(og, scx):
            idx = og * SC + scx
            kp = idx % 2
            vp = idx % 2
            for hl in range(4):
                oc = og * 4 + hl
                P.emit("tensor", lambda q, oc=oc, scx=scx, hl=hl, kp=kp, vp=vp: q.matmul(
                    ps_kv[kp][:, hl * 128:(hl + 1) * 128],
                    b.p2[oc][:, scx * 128:(scx + 1) * 128],
                    vt[vp][:, hl * 128:(hl + 1) * 128],
                    start=True, stop=True),
                    reads=[t.p2[oc], t_vt[vp]], writes=[t_pskv[kp]],
                    sig=(hl == 3))
            P.emit("vector", lambda q, og=og, kp=kp: q.tensor_tensor(
                b.kv_sb[:, og * 512:(og + 1) * 512],
                b.kv_sb[:, og * 512:(og + 1) * 512],
                ps_kv[kp][:], ALU.add),
                reads=[t_pskv[kp], t.kv[og]], writes=[t.kv[og]])

        P.dma("sync", lambda q: q.dma_start(out=b.brow2[:], in_=brow_in[1:2, :]),
              writes=[t.brow2], sem="Lbo")
        for og in range(4):
            wp = og % 2
            P.dma("sync", lambda q, og=og: q.dma_start(
                out=b.brow[:], in_=brow_in[0:1, og * 512:(og + 1) * 512]),
                writes=[t.brow], sem="Lbr")
            P.emit("tensor", lambda q: q.matmul(
                ps_rep[:], b.ones_row[:], b.brow[:], start=True, stop=True),
                reads=[t.brow, t.setup], writes=[t_psrep])
            P.emit("vector", lambda q, og=og: q.tensor_copy(
                b.brep[:, og * 512:(og + 1) * 512], ps_rep[:]),
                reads=[t_psrep], writes=[t.brep[og]])
            P.dma("gpsimd", lambda q, og=og, wp=wp: q.dma_start(
                out=wvb[wp][:], in_=wv_in[og]),
                writes=[t_wv[wp]], sem=f"Lwv{wp}")
            for scx in range(SC):
                vp = (og * SC + scx) % 2
                for fcx in range(FC):
                    P.emit("tensor", lambda q, wp=wp, fcx=fcx, scx=scx, vp=vp: q.matmul(
                        ps_v[vp][:], b.xn[fcx][:, scx * 128:(scx + 1) * 128],
                        wvb[wp][:, fcx * 512:(fcx + 1) * 512],
                        start=(fcx == 0), stop=(fcx == FC - 1)),
                        reads=[t_wv[wp], t.xn[fcx][scx // 4]], writes=[t_psv[vp]],
                        sig=(fcx == FC - 1))
                P.emit("vector", lambda q, og=og, vp=vp: q.tensor_tensor(
                    fv[vp][:], ps_v[vp][:], b.brep[:, og * 512:(og + 1) * 512], ALU.add),
                    reads=[t_psv[vp], t.brep[og]], writes=[t_fv[vp]])
                P.emit("scalar", lambda q, vp=vp: q.activation(vt[vp][:], fv[vp][:], AF.Silu),
                       reads=[t_fv[vp]], writes=[t_vt[vp]])
                if scx >= 1:
                    kv_part_mms(og, scx - 1)
            kv_part_mms(og, SC - 1)
        P.flush(blk)


def _phase_AB(P, nc, blk, b, t, wu_in, wo_in, u_scr, kv_part, kv_sum, brow_in, dbg, sb, psf):
    P.dma("gpsimd", lambda q: q.dma_start(out=kv_part.ap(), in_=b.kv_sb[:]),
          reads=[t.kv[0], t.kv[1], t.kv[2], t.kv[3]], sem="Skv")
    ev_cc = P.emit("gpsimd", lambda q: q.collective_compute(
        "AllReduce", ALU.add, replica_groups=[list(range(NCORE))],
        ins=[kv_part.ap().opt()], outs=[kv_sum.ap().opt()]),
        waits=[("Skv", 16)], sig=True, sem="cc", by=1, known=False)
    with ExitStack() as es:
        wub = [sb(es, "wub0", [128, D], BF16), sb(es, "wub1", [128, D], BF16)]
        fu = [sb(es, f"fuU{i}", [128, 512], F32) for i in range(2)]
        ps_u = [psf(es) for _ in range(2)]
        ps_rep = [psf(es) for _ in range(2)]
        t_rep = [Buf(), Buf()]
        for og in range(4):
            rp = og % 2
            P.emit("tensor", lambda q, og=og, rp=rp: q.matmul(
                ps_rep[rp][:], b.ones_row[:], b.brow2[0:1, og * 512:(og + 1) * 512],
                start=True, stop=True),
                reads=[t.brow2, t.setup], writes=[t_rep[rp]])
            P.emit("vector", lambda q, og=og, rp=rp: q.tensor_copy(
                b.brep[:, og * 512:(og + 1) * 512], ps_rep[rp][:]),
                reads=[t_rep[rp]], writes=[t.brep[og]])
        t_wu = [Buf(), Buf()]
        t_fu = [Buf(), Buf()]
        t_psu = [Buf(), Buf()]

        for oc in range(OC):
            wp = oc % 2
            P.dma("gpsimd", lambda q, oc=oc, wp=wp: q.dma_start(
                out=wub[wp][:], in_=wu_in[oc]),
                writes=[t_wu[wp]], sem=f"Lwu{wp}")
            P.dma("gpsimd", lambda q, oc=oc: q.dma_start(
                out=b.p2[oc][:], in_=wo_in[oc * 128:(oc + 1) * 128, :]),
                writes=[t.p2[oc]], sem="Lwo")
            for sg in range(SG):
                up = (oc * SG + sg) % 2
                for fcx in range(FC):
                    P.emit("tensor", lambda q, wp=wp, fcx=fcx, sg=sg, up=up: q.matmul(
                        ps_u[up][:], wub[wp][:, fcx * 128:(fcx + 1) * 128],
                        b.xn[fcx][:, sg * 512:(sg + 1) * 512],
                        start=(fcx == 0), stop=(fcx == FC - 1)),
                        reads=[t_wu[wp], t.xn[fcx][sg]], writes=[t_psu[up]],
                        sig=(fcx == FC - 1))
                P.emit("scalar", lambda q, oc=oc, up=up: q.activation(
                    fu[up][:], ps_u[up][:], AF.Silu,
                    bias=b.pp[:, PP_BU + oc:PP_BU + oc + 1]),
                    reads=[t_psu[up], t.pp], writes=[t_fu[up]])
                P.dma("scalar", lambda q, oc=oc, sg=sg, up=up: q.dma_start(
                    out=u_scr[oc].ap()[:, sg * 512:(sg + 1) * 512], in_=fu[up][:]),
                    reads=[t_fu[up]], sem=f"Su{up}")
        P.dma("gpsimd", lambda q: q.dma_start(out=b.kv_bf[:], in_=kv_sum.ap()),
              writes=[t.kvbf], waits=[ev_cc], sem="Lkv")
        if dbg:
            P.dma("gpsimd", lambda q: q.dma_start(out=dbg["kv"], in_=kv_sum.ap()),
                  sem="Sdbg")
        for ec in range(16):
            t.p2[ec].w = ("Lwo", 16 * 16)
            t.p2[ec].readers = []
        P.flush(blk)


def _phase_B(P, nc, blk, b, t, q_scr, u_scr, af_scr, brow_in, sb, psf):
    with ExitStack() as es:
        attn = [sb(es, f"attn{i}", [128, 512], BF16) for i in range(2)]
        qs = [sb(es, f"qsB{i}", [128, 512], BF16) for i in range(16)]
        b.rows = sb(es, "rowsB", [1, 2048], F32)
        ybuf = sb(es, "ybuf", [128, II], BF16)
        sq = [sb(es, f"sqB{i}", [128, 512], BF16) for i in range(2)]
        yt = [sb(es, f"ytB{i}", [128, 512], F32) for i in range(2)]
        fub = [sb(es, f"fubB{i}", [128, 512], F32) for i in range(2)]
        fab = [sb(es, f"fabB{i}", [128, 512], F32) for i in range(2)]
        rrep = sb(es, "rrepB", [128, 512], F32)
        nrep = sb(es, "nrepB", [128, 512], F32)
        ps_e2 = [psf(es) for _ in range(2)]
        ps_s1 = psf(es, (1, 512), F32)
        ps_s2 = psf(es, (1, 512), F32)
        ps_rp = psf(es)
        ps_wo = [psf(es) for _ in range(2)]

        t_attn = [Buf(), Buf()]
        t_qs = [Buf() for _ in range(16)]
        t_y = [Buf() for _ in range(16)]
        t_sq = [Buf(), Buf()]
        t_yt = [Buf(), Buf()]
        t_fub = [Buf(), Buf()]
        t_fab = [Buf(), Buf()]
        t_rrep = Buf()
        t_nrep = Buf()
        t_pse2 = [Buf(), Buf()]
        t_s1 = Buf()
        t_s2 = Buf()
        t_rp = Buf()
        t_wo = [Buf(), Buf()]
        t_row = [Buf() for _ in range(4)]

        R = lambda j: b.rows[0:1, j * 512:(j + 1) * 512]

        def wo_proj(sg, jr=range(16)):
            for j in jr:
                    sc_loc, og = j // 4, j % 4
                    wp = j % 2
                    for ec in range(16):
                        P.emit("tensor", lambda q, ec=ec, sc_loc=sc_loc, og=og, wp=wp: q.matmul(
                            ps_wo[wp][:],
                            ybuf[:, ec * 512 + sc_loc * 128: ec * 512 + sc_loc * 128 + 128],
                            b.p2[ec][:, og * 512:(og + 1) * 512],
                            start=(ec == 0), stop=(ec == 15)),
                            reads=[t_y[ec], t.p2[ec]], writes=[t_wo[wp]],
                            sig=(ec == 15))
                    P.emit("vector", lambda q, og=og, wp=wp, j=j: q.tensor_tensor(
                        fab[j % 2][:], ps_wo[wp][:], b.brep[:, og * 512:(og + 1) * 512], ALU.add),
                        reads=[t_wo[wp], t.brep[og]], writes=[t_fab[j % 2]])
                    P.dma("sync", lambda q, sg=sg, sc_loc=sc_loc, og=og, j=j: q.dma_start(
                        out=af_scr.ap()[sg * 512 + sc_loc * 128: sg * 512 + (sc_loc + 1) * 128,
                                        og * 512:(og + 1) * 512],
                        in_=fab[j % 2][:]),
                        reads=[t_fab[j % 2]], sem=f"Saf{j % 2}")

        def e2_mm(sg, h, first):
            ep = h % 2
            if first:
                P.dma("scalar", lambda q, h=h, sg=sg: q.dma_start(
                    out=qs[h][:], in_=q_scr[h].ap()[:, sg * 512:(sg + 1) * 512]),
                    writes=[t_qs[h]], sem=f"Lq{h}")
            P.emit("tensor", lambda q, h=h, ep=ep: q.matmul(
                ps_e2[ep][:], b.kv_bf[:, h * 128:(h + 1) * 128], qs[h][:],
                start=True, stop=True),
                reads=[t.kvbf, t_qs[h]], writes=[t_pse2[ep]])

        def stats_h(sg, h):
            ep = h % 2
            P.emit("vector", lambda q, ep=ep: q.tensor_copy(attn[ep][:], ps_e2[ep][:]),
                   reads=[t_pse2[ep]], writes=[t_attn[ep]])
            P.emit("vector", lambda q, ep=ep: q.tensor_tensor(sq[ep][:], attn[ep][:], attn[ep][:], ALU.mult),
                   reads=[t_attn[ep]], writes=[t_sq[ep]])
            P.emit("tensor", lambda q, h=h, ep=ep: q.matmul(
                ps_s1[:], b.ones_col_bf[:], attn[ep][:], start=(h == 0), stop=(h == 15)),
                reads=[t_attn[ep], t.setup], writes=[t_s1], sig=(h == 15))
            P.emit("tensor", lambda q, h=h, ep=ep: q.matmul(
                ps_s2[:], b.ones_col_bf[:], sq[ep][:], start=(h == 0), stop=(h == 15)),
                reads=[t_sq[ep]], writes=[t_s2], sig=(h == 15))

        def y_h(sg, h):
            ep = h % 2
            P.emit("vector", lambda q, ep=ep: q.tensor_tensor(
                yt[ep][:], ps_e2[ep][:], rrep[:], ALU.mult),
                reads=[t_pse2[ep], t_rrep], writes=[t_yt[ep]])
            P.emit("vector", lambda q, ep=ep: q.tensor_tensor(
                yt[ep][:], yt[ep][:], nrep[:], ALU.add),
                reads=[t_nrep], writes=[t_yt[ep]])
            P.emit("vector", lambda q, h=h, ep=ep: q.tensor_scalar(
                yt[ep][:], yt[ep][:],
                b.pp[:, PP_LNG + h:PP_LNG + h + 1], b.pp[:, PP_LNB + h:PP_LNB + h + 1],
                ALU.mult, ALU.add),
                reads=[t.pp], writes=[t_yt[ep]])
            P.dma("scalar", lambda q, h=h, sg=sg, ep=ep: q.dma_start(
                out=fub[ep][:], in_=u_scr[h].ap()[:, sg * 512:(sg + 1) * 512]),
                writes=[t_fub[ep]], sem=f"LuB{ep}")
            P.emit("vector", lambda q, h=h, ep=ep: q.tensor_tensor(
                ybuf[:, h * 512:(h + 1) * 512], yt[ep][:], fub[ep][:], ALU.mult),
                reads=[t_yt[ep], t_fub[ep]], writes=[t_y[h]])

        for sg in range(SG):
            for h in range(16):
                e2_mm(sg, h, True)
                if h >= 1:
                    stats_h(sg, h - 1)
            stats_h(sg, 15)
            if sg >= 1:
                wo_proj(sg - 1, range(8, 16))
            P.emit("scalar", lambda q: q.activation(R(0), ps_s1[:], AF.Copy, scale=1.0 / D),
                   reads=[t_s1], writes=[t_row[0]])
            P.emit("scalar", lambda q: q.activation(R(1), ps_s2[:], AF.Copy, scale=1.0 / D),
                   reads=[t_s2], writes=[t_row[1]])
            P.emit("scalar", lambda q: q.activation(R(2), R(0), AF.Square),
                   reads=[t_row[0]], writes=[t_row[2]])
            P.emit("vector", lambda q: q.tensor_tensor(R(3), R(1), R(2), ALU.subtract),
                   reads=[t_row[1], t_row[2]], writes=[t_row[3]])
            P.emit("scalar", lambda q: q.activation(R(2), R(3), AF.Ln, bias=b.eps_row[:]),
                   reads=[t_row[3], t.setup], writes=[t_row[2]])
            P.emit("scalar", lambda q: q.activation(R(1), R(2), AF.Exp, scale=-0.5),
                   reads=[t_row[2]], writes=[t_row[1]])
            P.emit("vector", lambda q: q.tensor_tensor(R(2), R(0), R(1), ALU.mult),
                   reads=[t_row[0], t_row[1]], writes=[t_row[2]])
            P.emit("vector", lambda q: q.tensor_scalar(R(3), R(2), -1.0, None, ALU.mult),
                   reads=[t_row[2]], writes=[t_row[3]])
            P.emit("tensor", lambda q: q.matmul(ps_rp[:], b.ones_row[:], R(1), start=True, stop=True),
                   reads=[t_row[1], t.setup], writes=[t_rp])
            P.emit("scalar", lambda q: q.copy(rrep[:], ps_rp[:]), reads=[t_rp], writes=[t_rrep])
            P.emit("tensor", lambda q: q.matmul(ps_rp[:], b.ones_row[:], R(3), start=True, stop=True),
                   reads=[t_row[3]], writes=[t_rp])
            P.emit("scalar", lambda q: q.copy(nrep[:], ps_rp[:]), reads=[t_rp], writes=[t_nrep])
            for h in range(16):
                e2_mm(sg, h, False)
                if h >= 1:
                    y_h(sg, h - 1)
            y_h(sg, 15)
            wo_proj(sg, range(0, 8))
        wo_proj(SG - 1, range(8, 16))
        P.flush(blk)


def _phase_F(P, nc, blk, b, t, x_in, guw_in, wd_in, af_scr, out_t, sb, psf):
    with ExitStack() as es:
        guw = [sb(es, f"guwF{i}", [128, 4096], BF16) for i in range(2)]
        wdb = [sb(es, f"wdF{i}", [128, II // 2], BF16) for i in range(3)]
        ftm = [sb(es, f"ftm{j}", [128, 512], F32) for j in range(4)]
        fgc = [sb(es, f"fgc{i}", [128, 512], F32) for i in range(2)]
        fuc = [sb(es, f"fuc{i}", [128, 512], F32) for i in range(2)]
        fds = [sb(es, f"fds{i}", [128, 512], F32) for i in range(2)]
        fxc = [sb(es, f"fxc{i}", [128, 512], F32) for i in range(2)]
        fac = [sb(es, f"fac{i}", [128, 512], F32) for i in range(2)]
        rs_rep = [sb(es, f"rs_rep{i}", [128, 512], F32) for i in range(2)]
        cs_rep = sb(es, "cs_rep", [128, 512], F32)
        rowf = sb(es, "rowf", [1, 512], F32)
        ident_f = sb(es, "ident_fF", [128, 128], F32)
        t_identf = Buf()
        P.emit("gpsimd", lambda q: q.memset(ident_f[:], 0.0), writes=[t_identf])
        P.emit("gpsimd", lambda q: q.affine_select(
            out=ident_f[:], in_=ident_f[:], compare_op=ALU.not_equal,
            fill=1.0, base=0, pattern=[[-1, 128]], channel_multiplier=1), writes=[t_identf])
        ps_g = [psf(es) for _ in range(2)]
        ps_u = [psf(es) for _ in range(2)]
        ps_d = [psf(es) for _ in range(2)]
        ps_sc = psf(es, (1, 512), F32)
        ps_t = psf(es)

        t_guw = [Buf(), Buf()]
        t_wd = [Buf(), Buf(), Buf()]
        t_ftm = [Buf() for _ in range(4)]
        t_fgc = [Buf(), Buf()]
        t_fuc = [Buf(), Buf()]
        t_fds = [Buf(), Buf()]
        t_fxc = [Buf(), Buf()]
        t_fac = [Buf(), Buf()]
        t_rsrep = [Buf(), Buf()]
        t_csrep = Buf()
        t_psg = [Buf(), Buf()]
        t_psu = [Buf(), Buf()]
        t_psd = [Buf(), Buf()]
        t_pssc = Buf()
        t_pst = Buf()
        t_act = [Buf() for _ in range(ICN)]
        t_rowF = Buf()

        R = lambda j: rowf[:]
        act_ap = lambda ic: b.p2[ic // 4][:, (ic % 4) * 512:(ic % 4) * 512 + 512]

        def evac_hc(sg, hc):
            dp = hc % 2
            hg, hl = hc // 4, hc % 4
            P.emit("vector", lambda q, dp=dp: q.tensor_tensor(
                fds[dp][:], ps_d[dp][:], cs_rep[:], ALU.subtract),
                reads=[t_psd[dp], t_csrep], writes=[t_fds[dp]])
            P.emit("scalar", lambda q, hc=hc, dp=dp: q.activation(
                fds[dp][:], fds[dp][:], AF.Copy,
                scale=b.pp[:, PP_DS + hc:PP_DS + hc + 1]),
                reads=[t.pp], writes=[t_fds[dp]])
            for j in range(4):
                P.emit("tensor", lambda q, dp=dp, j=j: q.transpose(
                    ps_t[:, j * 128:(j + 1) * 128],
                    fds[dp][:, j * 128:(j + 1) * 128], ident_f[:]),
                    reads=[t_fds[dp], t_identf], writes=[t_pst], sig=(j == 3))
            for j in range(4):
                P.emit("scalar", lambda q, j=j, hl=hl: q.copy(
                    ftm[j][:, hl * 128:(hl + 1) * 128],
                    ps_t[:, j * 128:(j + 1) * 128]),
                    reads=[t_pst], writes=[t_ftm[j]], sig=(j == 3))

        def adds_hg(sg, hg):
            for j in range(4):
                xp = j % 2
                rowa = sg * 512 + j * 128
                P.dma("sync", lambda q, rowa=rowa, hg=hg, xp=xp: q.dma_start(
                    out=fxc[xp][:], in_=x_in[rowa:rowa + 128, hg * 512:(hg + 1) * 512]),
                    writes=[t_fxc[xp]], sem=f"Lxf{xp}")
                P.dma("sync", lambda q, rowa=rowa, hg=hg, xp=xp: q.dma_start(
                    out=fac[xp][:], in_=af_scr.ap()[rowa:rowa + 128, hg * 512:(hg + 1) * 512]),
                    writes=[t_fac[xp]], sem=f"Laf{xp}")
                P.emit("vector", lambda q, j=j, xp=xp: q.tensor_tensor(
                    ftm[j][:], ftm[j][:], fxc[xp][:], ALU.add),
                    reads=[t_fxc[xp]], writes=[t_ftm[j]])
                P.emit("vector", lambda q, j=j, xp=xp: q.tensor_tensor(
                    ftm[j][:], ftm[j][:], fac[xp][:], ALU.add),
                    reads=[t_fac[xp]], writes=[t_ftm[j]])
                P.dma("scalar", lambda q, j=j, hg=hg, rowa=rowa: q.dma_start(
                    out=out_t[rowa:rowa + 128, hg * 512:(hg + 1) * 512],
                    in_=ftm[j][:]),
                    reads=[t_ftm[j]], sem=f"So{j}")

        def rs_chain(sg2):
            rp = sg2 % 2
            cols2 = slice(sg2 * 512, (sg2 + 1) * 512)
            for fcx in range(FC):
                P.emit("tensor", lambda q, fcx=fcx, cols2=cols2: q.matmul(
                    ps_sc[:], b.ones_col_bf[:], b.xn[fcx][:, cols2],
                    start=(fcx == 0), stop=(fcx == FC - 1)),
                    reads=[t.xn[fcx][sg2], t.setup], writes=[t_pssc], sig=(fcx == FC - 1))
            P.emit("scalar", lambda q: q.activation(R(0), ps_sc[:], AF.Copy, scale=128.0),
                   reads=[t_pssc], writes=[t_rowF])
            P.emit("tensor", lambda q: q.matmul(ps_t[:], b.ones_row[:], R(0), start=True, stop=True),
                   reads=[t_rowF, t.setup], writes=[t_pst])
            P.emit("scalar", lambda q, rp=rp: q.copy(rs_rep[rp][:], ps_t[:]),
                   reads=[t_pst], writes=[t_rsrep[rp]])

        for sg in range(SG):
            cols = slice(sg * 512, (sg + 1) * 512)
            if sg == 0:
                rs_chain(0)
            for ic in range(ICN):
                gp = ic % 2
                P.dma("gpsimd", lambda q, ic=ic, gp=gp: q.dma_start(
                    out=guw[gp][:], in_=guw_in[ic]),
                    writes=[t_guw[gp]], sem=f"Lgu{gp}")
                for fcx in range(FC):
                    P.emit("tensor", lambda q, gp=gp, fcx=fcx, cols=cols: q.matmul(
                        ps_g[gp][:], guw[gp][:, fcx * 256:fcx * 256 + 128],
                        b.xn[fcx][:, cols], start=(fcx == 0), stop=(fcx == FC - 1)),
                        reads=[t_guw[gp], t.xn[fcx][sg]], writes=[t_psg[gp]], sig=(fcx == FC - 1))
                for fcx in range(FC):
                    P.emit("tensor", lambda q, gp=gp, fcx=fcx, cols=cols: q.matmul(
                        ps_u[gp][:], guw[gp][:, fcx * 256 + 128:fcx * 256 + 256],
                        b.xn[fcx][:, cols], start=(fcx == 0), stop=(fcx == FC - 1)),
                        reads=[t_guw[gp], t.xn[fcx][sg]], writes=[t_psu[gp]], sig=(fcx == FC - 1))
                P.emit("vector", lambda q, gp=gp, sg=sg: q.tensor_tensor(
                    fgc[gp][:], ps_g[gp][:], rs_rep[sg % 2][:], ALU.subtract),
                    reads=[t_psg[gp], t_rsrep[sg % 2]], writes=[t_fgc[gp]])
                P.emit("scalar", lambda q, ic=ic, gp=gp: q.activation(
                    fgc[gp][:], fgc[gp][:], AF.Silu,
                    scale=b.pp[:, PP_GS + ic:PP_GS + ic + 1]),
                    reads=[t.pp], writes=[t_fgc[gp]])
                P.emit("vector", lambda q, gp=gp, sg=sg: q.tensor_tensor(
                    fuc[gp][:], ps_u[gp][:], rs_rep[sg % 2][:], ALU.subtract),
                    reads=[t_psu[gp], t_rsrep[sg % 2]], writes=[t_fuc[gp]])
                P.emit("vector", lambda q, ic=ic, gp=gp: q.scalar_tensor_tensor(
                    act_ap(ic), fuc[gp][:], b.pp[:, PP_US + ic:PP_US + ic + 1], fgc[gp][:],
                    ALU.mult, ALU.mult),
                    reads=[t_fuc[gp], t_fgc[gp], t.pp], writes=[t_act[ic]])
            for ic in range(ICN):
                P.emit("tensor", lambda q, ic=ic: q.matmul(
                    ps_sc[:], b.ones_col_bf[:], act_ap(ic),
                    start=(ic == 0), stop=(ic == ICN - 1)),
                    reads=[t_act[ic], t.setup], writes=[t_pssc], sig=(ic == ICN - 1))
            P.emit("scalar", lambda q: q.activation(R(0), ps_sc[:], AF.Copy, scale=128.0),
                   reads=[t_pssc], writes=[t_rowF])
            P.emit("tensor", lambda q: q.matmul(ps_t[:], b.ones_row[:], R(0), start=True, stop=True),
                   reads=[t_rowF], writes=[t_pst])
            P.emit("scalar", lambda q: q.copy(cs_rep[:], ps_t[:]), reads=[t_pst], writes=[t_csrep])
            if sg + 1 < SG:
                rs_chain(sg + 1)
            for hc in range(HCN):
                dp = hc % 2
                for k in range(2):
                    hp = (hc * 2 + k) % 3
                    P.dma("gpsimd", lambda q, hc=hc, k=k, hp=hp: q.dma_start(
                        out=wdb[hp][:], in_=wd_in[hc, :, k * 4096:(k + 1) * 4096]),
                        writes=[t_wd[hp]], sem=f"Lwd{hp}")
                for ic in range(ICN):
                    hp = (hc * 2 + ic // 32) % 3
                    P.emit("tensor", lambda q, dp=dp, hp=hp, ic=ic: q.matmul(
                        ps_d[dp][:],
                        wdb[hp][:, (ic % 32) * 128:(ic % 32 + 1) * 128], act_ap(ic),
                        start=(ic == 0), stop=(ic == ICN - 1)),
                        reads=[t_wd[hp], t_act[ic]], writes=[t_psd[dp]],
                        sig=(ic == ICN - 1))
                if hc >= 1:
                    evac_hc(sg, hc - 1)
                    if hc % 4 == 0:
                        adds_hg(sg, hc // 4 - 1)
            evac_hc(sg, HCN - 1)
            adds_hg(sg, 3)
        fin = [(f"So{j}", P.cnt.get(f"So{j}", 0)) for j in range(4)]
        if "Sdbg" in P.cnt:
            fin.append(("Sdbg", P.cnt["Sdbg"]))
        P.emit("sync", lambda q: q.nop(), waits=fin, sig=False)
        P.flush(blk)



def _repack_u8(a_int32):
    a = np.ascontiguousarray(np.asarray(a_int32))
    assert a.dtype == np.int32
    return np.ascontiguousarray(a.view(np.uint8).reshape(*a.shape, 4)[..., 0])


def _prep_inputs(inp):
    f = lambda v: np.ascontiguousarray(np.asarray(v, dtype=np.float32))
    x = f(inp["L0_attn_x"]).reshape(NCORE, S, D)
    wqk = np.ascontiguousarray(
        f(inp["L0_attn_Wqk"]).reshape(16, 128, OC, 128).transpose(2, 1, 0, 3).reshape(OC, 128, D))
    wv = np.ascontiguousarray(
        f(inp["L0_attn_Wv"]).reshape(16, 128, 4, 512).transpose(2, 1, 0, 3).reshape(4, 128, II))
    wu = np.ascontiguousarray(
        f(inp["L0_attn_Wu"]).reshape(16, 128, OC, 128).transpose(2, 1, 0, 3).reshape(OC, 128, D))
    wo = f(inp["L0_attn_Wo"])
    pp = np.zeros((128, PP_COLS), np.float32)
    pp[:, PP_BQK:PP_BQK + 16] = f(inp["L0_attn_bqk"]).reshape(16, 128).T
    pp[:, PP_BU:PP_BU + 16] = f(inp["L0_attn_bu"]).reshape(16, 128).T
    pp[:, PP_LNG:PP_LNG + 16] = f(inp["L0_attn_ln_g"]).reshape(16, 128).T
    pp[:, PP_LNB:PP_LNB + 16] = f(inp["L0_attn_ln_b"]).reshape(16, 128).T
    pp[:, PP_GS:PP_GS + 64] = f(inp["L0_ffn_gate_scale"]).reshape(64, 128).T
    pp[:, PP_US:PP_US + 64] = f(inp["L0_ffn_up_scale"]).reshape(64, 128).T
    pp[:, PP_DS:PP_DS + 16] = f(inp["L0_ffn_down_scale"]).reshape(16, 128).T
    brow = np.ascontiguousarray(np.stack([f(inp["L0_attn_bv"]), f(inp["L0_attn_bo"])]))
    g8 = _repack_u8(inp["L0_ffn_gate_w"])
    u8 = _repack_u8(inp["L0_ffn_up_w"])
    gt = g8.T.reshape(16, 128, ICN, 128)
    ut = u8.T.reshape(16, 128, ICN, 128)
    guw = np.ascontiguousarray(
        np.concatenate([gt[:, :, :, None, :], ut[:, :, :, None, :]], axis=3)
        .transpose(2, 1, 0, 3, 4).reshape(ICN, 128, 4096))
    d8 = _repack_u8(inp["L0_ffn_down_w"])
    wd = np.ascontiguousarray(
        d8.T.reshape(ICN, 128, HCN, 128).transpose(2, 1, 0, 3).reshape(HCN, 128, II))
    const = {"wqk": wqk, "wv": wv, "wu": wu, "wo": wo, "pp": pp, "brow": brow,
             "guw": guw, "wd": wd}
    return x, const


def kernel(**inputs):
    x, const = _prep_inputs(inputs)
    nc = build(stage=3, debug=False)
    in_maps = [dict(const, x=np.ascontiguousarray(x[c])) for c in range(NCORE)]
    res = run_bass_kernel_spmd(nc, in_maps, core_ids=list(range(NCORE)))
    out = np.concatenate([np.asarray(res.results[c]["out"]) for c in range(NCORE)], axis=0)
    return out.reshape(1, NCORE * S, D).astype(np.float32)


if __name__ == "__main__":
    import time
    t0 = time.time()
    nc = build(stage=3, debug=False)
    print(f"built in {time.time() - t0:.1f}s")


# version 35
# speedup vs baseline: 1.0045x; 1.0045x over previous
import numpy as np
from contextlib import ExitStack

import concourse.bass as bass
import concourse.mybir as mybir
from concourse.bass_utils import run_bass_kernel_spmd

F32 = mybir.dt.float32
BF16 = mybir.dt.bfloat16
U8 = mybir.dt.uint8
AF = mybir.ActivationFunctionType
ALU = mybir.AluOpType
AX = mybir.AxisListType

NCORE = 8
S = 2048
D = 2048
II = 8192
FC = 16
OC = 16
SC = 16
SG = 4
ICN = 64
HCN = 16
LN_EPS = 1e-5

PP_BQK, PP_BU, PP_LNG, PP_LNB, PP_GS, PP_US, PP_DS = 0, 16, 32, 48, 64, 128, 192
PP_COLS = 208

ENGS = ("tensor", "scalar", "vector", "gpsimd", "sync")
ENG_SEM = {"tensor": "pe", "scalar": "act", "vector": "dve", "gpsimd": "pool", "sync": "syn"}


class Buf:
    __slots__ = ("w", "readers")

    def __init__(self):
        self.w = None
        self.readers = []


class Sched:

    def __init__(self, nc, es_sem):
        self.nc = nc
        self.es_sem = es_sem
        self.semh = {}
        self.cnt = {}
        self.known = {e: {} for e in ENGS}
        self.ops = {e: [] for e in ENGS}
        self.pend = {e: [] for e in ENGS}
        self.nops = 0
        for e in ENGS:
            self.sem(ENG_SEM[e])

    def sem(self, name):
        if name not in self.semh:
            self.semh[name] = self.es_sem.enter_context(self.nc.semaphore(name))
            self.cnt[name] = 0
        return name

    def emit(self, eng, fn, reads=(), writes=(), waits=(), sig=True, sem=None,
             by=1, known=True, is_dma=False):
        evs = []
        for bf in reads:
            if bf.w:
                evs.append(bf.w)
        for bf in writes:
            if bf.w:
                evs.append(bf.w)
            evs.extend(bf.readers)
        evs.extend(w for w in waits if w)
        kn = self.known[eng]
        need = {}
        for sname, val in evs:
            if val > 0 and need.get(sname, 0) < val:
                need[sname] = val
        wl = []
        own = ENG_SEM[eng]
        for sname, val in need.items():
            if kn.get(sname, 0) < val or (is_dma and sname == own):
                wl.append((sname, val))
                kn[sname] = val
        ev = None
        if sig:
            sname = sem if sem else ENG_SEM[eng]
            self.sem(sname)
            self.cnt[sname] += by
            ev = (sname, self.cnt[sname])
            if known:
                kn[sname] = self.cnt[sname]
        self.ops[eng].append((wl, fn, None if ev is None else ev[0], by))
        self.nops += 1
        if ev is not None:
            for bf in reads:
                bf.readers.append(ev)
            for bf in writes:
                bf.w = ev
                bf.readers = []
            for bf, isw in self.pend[eng]:
                if isw:
                    bf.w = ev
                    bf.readers = []
                else:
                    bf.readers.append(ev)
            self.pend[eng] = []
        else:
            for bf in reads:
                self.pend[eng].append((bf, False))
            for bf in writes:
                self.pend[eng].append((bf, True))
        return ev

    def dma(self, eng, fn, reads=(), writes=(), waits=(), sem="dma"):
        return self.emit(eng, fn, reads=reads, writes=writes, waits=waits,
                         sig=True, sem=sem, by=16, known=False, is_dma=True)

    def barrier_evs(self):
        return [(s, c) for s, c in self.cnt.items() if c > 0]

    def barrier(self):
        evs = self.barrier_evs()
        for e in ENGS:
            self.emit(e, lambda q: q.nop(), waits=evs, sig=False)

    def flush(self, blk):
        for eng in ENGS:
            ops = self.ops[eng]
            if not ops:
                continue
            self.ops[eng] = []
            sec = getattr(blk, eng)

            def body(q, ops=ops):
                for wl, fn, sname, by in ops:
                    for wn, wv in wl:
                        q.wait_ge(self.semh[wn], wv)
                    ins = fn(q)
                    if sname is not None:
                        ins.then_inc(self.semh[sname], by)
            sec(body)


class NS:
    pass


def _blk3(dst_ap, src2d, p=128):
    c = src2d.shape[0] // p
    return dst_ap.rearrange("p (c w) -> p c w", c=c), src2d.rearrange("(c p) w -> p c w", p=p)




def build(stage=3, debug=False):
    nc = bass.Bass("TRN2", target_bir_lowering=False)

    x_in = nc.dram_tensor("x", [S, D], F32, kind="ExternalInput").ap()
    wqk_in = nc.dram_tensor("wqk", [OC, 128, D], F32, kind="ExternalInput").ap()
    wv_in = nc.dram_tensor("wv", [4, 128, II], F32, kind="ExternalInput").ap()
    wu_in = nc.dram_tensor("wu", [OC, 128, D], F32, kind="ExternalInput").ap()
    wo_in = nc.dram_tensor("wo", [D, D], F32, kind="ExternalInput").ap()
    pp_in = nc.dram_tensor("pp", [128, PP_COLS], F32, kind="ExternalInput").ap()
    brow_in = nc.dram_tensor("brow", [2, D], F32, kind="ExternalInput").ap()
    guw_in = nc.dram_tensor("guw", [ICN, 128, 4096], U8, kind="ExternalInput").ap()
    wd_in = nc.dram_tensor("wd", [HCN, 128, II], U8, kind="ExternalInput").ap()
    out_t = nc.dram_tensor("out", [S, D], F32, kind="ExternalOutput").ap()

    q_scr = [nc.dram_tensor(f"q_scr{i}", [128, S], BF16) for i in range(OC)]
    u_scr = [nc.dram_tensor(f"u_scr{i}", [128, S], F32) for i in range(OC)]
    af_scr = nc.dram_tensor("af_scr", [S, D], F32)
    kv_part = nc.dram_tensor("kv_part", [128, D], F32)
    kv_sum = nc.dram_tensor("kv_sum", [128, D], F32)

    dbg = {}
    if debug:
        dbg["kv"] = nc.dram_tensor("dbg_kv", [128, D], F32, kind="ExternalOutput").ap()
        dbg["xn0"] = nc.dram_tensor("dbg_xn0", [128, S], BF16, kind="ExternalOutput").ap()
        dbg["q0"] = nc.dram_tensor("dbg_q0", [128, S], BF16, kind="ExternalOutput").ap()
        dbg["u0"] = nc.dram_tensor("dbg_u0", [128, S], F32, kind="ExternalOutput").ap()
        dbg["af"] = nc.dram_tensor("dbg_af", [S, D], F32, kind="ExternalOutput").ap()

    with ExitStack() as es0:
        P = Sched(nc, es0)
        b = NS()
        sb = lambda es, name, shape, dt: es.enter_context(nc.sbuf_tensor(name, shape, dt))
        _psn = [0]
        def psf(es, shape=(128, 512), dt=F32):
            _psn[0] += 1
            return es.enter_context(nc.psum_tensor(f'ps{_psn[0]}', list(shape), dt))

        b.xn = [sb(es0, f"xn{i}", [128, S], BF16) for i in range(FC)]
        b.p2 = [sb(es0, f"p2_{i}", [128, S], BF16) for i in range(16)]
        b.pp = sb(es0, "pp_sb", [128, PP_COLS], F32)
        b.ident_bf = sb(es0, "ident_bf", [128, 128], BF16)
        b.ones_col = sb(es0, "ones_col", [128, 1], F32)
        b.ones_col_bf = sb(es0, "ones_col_bf", [128, 1], BF16)
        b.ones_row = sb(es0, "ones_row", [1, 128], F32)
        b.eps_col = sb(es0, "eps_col", [128, 1], F32)
        b.eps_row = sb(es0, "eps_row", [1, 1], F32)
        b.stat = sb(es0, "stat", [128, 16], F32)
        b.nbqk = sb(es0, "nbqk", [128, 16], F32)

        t = NS()
        t.xn = [[Buf() for _ in range(4)] for _ in range(FC)]
        t.p2 = [Buf() for _ in range(16)]
        t.pp = Buf()
        t.setup = Buf()
        t.stat = [Buf(), Buf()]

        blk = es0.enter_context(nc.Block())

        P.emit("gpsimd", lambda q: q.memset(b.ident_bf[:], 0.0), writes=[t.setup])
        P.emit("gpsimd", lambda q: q.affine_select(
            out=b.ident_bf[:], in_=b.ident_bf[:], compare_op=ALU.not_equal,
            fill=1.0, base=0, pattern=[[-1, 128]], channel_multiplier=1), writes=[t.setup])
        P.emit("gpsimd", lambda q: q.memset(b.ones_col[:], 1.0), writes=[t.setup])
        P.emit("gpsimd", lambda q: q.memset(b.ones_col_bf[:], 1.0), writes=[t.setup])
        P.emit("gpsimd", lambda q: q.memset(b.ones_row[:], 1.0), writes=[t.setup])
        P.emit("gpsimd", lambda q: q.memset(b.eps_col[:], LN_EPS), writes=[t.setup])
        P.emit("gpsimd", lambda q: q.memset(b.eps_row[:], LN_EPS), writes=[t.setup])
        P.dma("sync", lambda q: q.dma_start(out=b.pp[:], in_=pp_in), writes=[t.pp], sem="Lpp")
        t.nbqk = Buf()
        P.emit("vector", lambda q: q.tensor_scalar(
            b.nbqk[:], b.pp[:, PP_BQK:PP_BQK + 16], -1.0, None, ALU.mult),
            reads=[t.pp], writes=[t.nbqk])

        with ExitStack() as es1:
            b.brep = sb(es1, "brep", [128, D], BF16)
            b.brow = sb(es1, "brow_v", [1, 512], F32)
            b.kv_bf = sb(es1, "kv_bf", [128, D], BF16)

            t.brep = [Buf() for _ in range(4)]
            t.brow = Buf()
            t.brow2 = Buf()
            t.kv = [Buf() for _ in range(4)]
            t.kvbf = Buf()

            with ExitStack() as esA01:
                _phase_A0(P, nc, blk, b, t, x_in, sb, psf, esA01)
                if debug:
                    P.dma("sync", lambda q: q.dma_start(out=dbg["xn0"], in_=b.xn[0][:]),
                          reads=t.xn[0], sem="Sdbg")
                _phase_A1(P, nc, blk, b, t, wqk_in, q_scr, sb, psf, esA01)
            P.barrier()
            if debug:
                P.dma("sync", lambda q: q.dma_start(out=dbg["q0"], in_=q_scr[0].ap()),
                      sem="Sdbg")
            with ExitStack() as esKV:
                b.kv_sb = sb(esKV, "kv_sb", [128, D], F32)
                b.brow2 = sb(esKV, "brow_o", [1, D], F32)
                P.emit("gpsimd", lambda q: q.memset(b.kv_sb[:], 0.0),
                       writes=[t.kv[0], t.kv[1], t.kv[2], t.kv[3]])
                _phase_A2(P, nc, blk, b, t, wv_in, brow_in, sb, psf)
                P.barrier()
                _phase_AB(P, nc, blk, b, t, wu_in, wo_in, u_scr, kv_part, kv_sum, brow_in, dbg, sb, psf)
                P.barrier()
                P.flush(blk)
            if debug:
                P.dma("sync", lambda q: q.dma_start(out=dbg["u0"], in_=u_scr[0].ap()),
                      sem="Sdbg")
                P.flush(blk)
            if stage >= 2:
                _phase_B(P, nc, blk, b, t, q_scr, u_scr, af_scr, brow_in, sb, psf)
                P.barrier()
                if debug:
                    P.dma("sync", lambda q: q.dma_start(out=dbg["af"], in_=af_scr.ap()),
                          sem="Sdbg")
                    P.flush(blk)
            P.flush(blk)
        if stage >= 3:
            _phase_F(P, nc, blk, b, t, x_in, guw_in, wd_in, af_scr, out_t, sb, psf)
        else:
            P.dma("sync", lambda q: q.dma_start(out=out_t[0:128, 0:PP_COLS], in_=b.pp[:]),
                  reads=[t.pp], waits=P.barrier_evs(), sem="Sfin")
            fin = [("Sfin", 16)]
            if "Sdbg" in P.cnt:
                fin.append(("Sdbg", P.cnt["Sdbg"]))
            P.emit("sync", lambda q: q.nop(), waits=fin, sig=False)
        P.flush(blk)

    return nc


def _phase_A0(P, nc, blk, b, t, x_in, sb, psf, es_sh):
    if True:
        es = es_sh
        xrow = [sb(es, f"xrow{i}", [128, D], F32) for i in range(2)]
        xsq = [sb(es, f"xsq{i}", [128, D], F32) for i in range(2)]
        xnr = [sb(es, f"xnr{i}", [128, D], BF16) for i in range(2)]
        ps_tr = [psf(es, (128, 1024), BF16) for _ in range(4)]
        t_xrow = [Buf(), Buf(), Buf()]
        t_xsq = [Buf(), Buf()]
        t_xnr = [Buf(), Buf()]
        t_xnrh = [Buf(), Buf()]
        t_pstr = [Buf(), Buf(), Buf(), Buf()]

        def spread(i):
            sp = i % 2
            for fcx in range(FC):
                tp = (i * FC + fcx) % 4
                th = t_xnr[sp] if fcx < 8 else t_xnrh[sp]
                P.emit("tensor", lambda q, sp=sp, fcx=fcx, tp=tp: q.transpose(
                    ps_tr[tp][:, 0:128], xnr[sp][:, fcx * 128:(fcx + 1) * 128], b.ident_bf[:]),
                    reads=[th, t.setup], writes=[t_pstr[tp]])
                P.emit("scalar", lambda q, i=i, fcx=fcx, tp=tp: q.copy(
                    b.xn[fcx][:, i * 128:(i + 1) * 128], ps_tr[tp][:, 0:128]),
                    reads=[t_pstr[tp]], writes=[t.xn[fcx][i // 4]])

        for i in range(SC):
            p = i % 2
            sp = i % 2
            P.dma("sync", lambda q, p=p, i=i: q.dma_start(
                out=xrow[p][:], in_=x_in[i * 128:(i + 1) * 128, :]),
                writes=[t_xrow[p]], sem=f"Lx{p}")
            o = 8 * sp
            C = lambda j: b.stat[:, o + j:o + j + 1]
            P.emit("scalar", lambda q, p=p, sp=sp: q.activation(xsq[sp][:], xrow[p][:], AF.Square),
                   reads=[t_xrow[p]], writes=[t_xsq[sp]])
            P.emit("vector", lambda q, p=p, C=C: q.reduce_sum(C(0), xrow[p][:], axis=AX.X),
                   reads=[t_xrow[p]], writes=[t.stat[sp]], sig=False)
            P.emit("vector", lambda q, sp=sp, C=C: q.reduce_sum(C(1), xsq[sp][:], axis=AX.X),
                   reads=[t_xsq[sp]], writes=[t.stat[sp]], sig=False)
            if i >= 1:
                spread(i - 1)
            P.emit("vector", lambda q, C=C: q.tensor_scalar(C(2), C(0), 1.0 / D, None, ALU.mult), sig=False)
            P.emit("vector", lambda q, C=C: q.tensor_scalar(C(3), C(1), 1.0 / D, None, ALU.mult),
                   writes=[t.stat[sp]])
            P.emit("scalar", lambda q, C=C: q.activation(C(4), C(2), AF.Square),
                   reads=[t.stat[sp]], writes=[t.stat[sp]])
            P.emit("vector", lambda q, C=C: q.tensor_tensor(C(5), C(3), C(4), ALU.subtract),
                   reads=[t.stat[sp]], writes=[t.stat[sp]])
            P.emit("scalar", lambda q, C=C: q.activation(C(6), C(5), AF.Ln, bias=b.eps_col[:]),
                   reads=[t.stat[sp], t.setup], sig=False)
            P.emit("scalar", lambda q, C=C: q.activation(C(4), C(6), AF.Exp, scale=-0.5),
                   writes=[t.stat[sp]])
            P.emit("vector", lambda q, C=C: q.tensor_scalar(C(7), C(2), C(4), -1.0, ALU.mult, ALU.mult),
                   reads=[t.stat[sp]], writes=[t.stat[sp]])
            P.emit("vector", lambda q, p=p, sp=sp, C=C: q.tensor_scalar(
                xnr[sp][:, 0:1024], xrow[p][:, 0:1024], C(4), C(7), ALU.mult, ALU.add),
                reads=[t_xrow[p], t.stat[sp]], writes=[t_xnr[sp]])
            P.emit("gpsimd", lambda q, p=p, sp=sp, C=C: q.tensor_scalar(
                xnr[sp][:, 1024:2048], xrow[p][:, 1024:2048], C(4), C(7), ALU.mult, ALU.add),
                reads=[t_xrow[p], t.stat[sp]], writes=[t_xnrh[sp]])
        spread(SC - 1)
        P.flush(blk)


def _phase_A1(P, nc, blk, b, t, wqk_in, q_scr, sb, psf, es_sh):
    if True:
        es = es_sh
        wq = [sb(es, f"wqA{i}", [128, D], BF16) for i in range(2)]
        qstage = [sb(es, f"qstage{i}", [128, S], BF16) for i in range(2)]
        fm = [sb(es, f"fmA{i}", [128, 512], F32) for i in range(2)]
        fp = [sb(es, f"fpA{i}", [128, 512], F32) for i in range(2)]
        ps_q = [psf(es) for _ in range(2)]
        ps_tr = [psf(es, (128, 1024), BF16) for _ in range(2)]
        t_wq = [Buf(), Buf()]
        t_qst = [Buf(), Buf()]
        t_fm = [Buf(), Buf()]
        t_fp = [Buf(), Buf()]
        t_psq = [Buf(), Buf()]
        t_pstr = [Buf(), Buf()]

        def transposes(oc, grp=None):
            rng = range(SC) if grp is None else range(grp * 4, grp * 4 + 4)
            for scx in rng:
                tp = (oc * SC + scx) % 2
                P.emit("tensor", lambda q, oc=oc, scx=scx, tp=tp: q.transpose(
                    ps_tr[tp][:, 0:128], qstage[oc % 2][:, scx * 128:(scx + 1) * 128],
                    b.ident_bf[:]),
                    reads=[t_qst[oc % 2]], writes=[t_pstr[tp]])
                P.emit("vector", lambda q, oc=oc, scx=scx, tp=tp: q.tensor_copy(
                    b.p2[oc][:, scx * 128:(scx + 1) * 128], ps_tr[tp][:, 0:128]),
                    reads=[t_pstr[tp]], writes=[t.p2[oc]])

        for oc in range(OC):
            wp = oc % 2
            P.dma("gpsimd", lambda q, oc=oc, wp=wp: q.dma_start(
                out=wq[wp][:], in_=wqk_in[oc]),
                writes=[t_wq[wp]], sem=f"Lwq{wp}")
            for sg in range(SG):
                qp = (oc * SG + sg) % 2
                for fcx in range(FC):
                    P.emit("tensor", lambda q, wp=wp, fcx=fcx, sg=sg, qp=qp: q.matmul(
                        ps_q[qp][:], wq[wp][:, fcx * 128:(fcx + 1) * 128],
                        b.xn[fcx][:, sg * 512:(sg + 1) * 512],
                        start=(fcx == 0), stop=(fcx == FC - 1)),
                        reads=[t_wq[wp], t.xn[fcx][sg]], writes=[t_psq[qp]],
                        sig=(fcx == FC - 1))
                if oc >= 1:
                    transposes(oc - 1, sg)
                bias = b.pp[:, PP_BQK + oc:PP_BQK + oc + 1]
                nbias = b.nbqk[:, oc:oc + 1]
                P.emit("scalar", lambda q, qp=qp, nbias=nbias: q.activation(
                    fm[qp][:], ps_q[qp][:], AF.Relu, bias=nbias, scale=-1.0),
                    reads=[t_psq[qp], t.nbqk], writes=[t_fm[qp]])
                P.emit("scalar", lambda q, qp=qp, bias=bias: q.activation(
                    fp[qp][:], ps_q[qp][:], AF.Relu, bias=bias),
                    reads=[t_psq[qp], t.pp], writes=[t_fp[qp]])
                P.emit("scalar", lambda q, qp=qp: q.activation(fm[qp][:], fm[qp][:], AF.Exp, scale=-1.0),
                       writes=[t_fm[qp]])
                P.emit("vector", lambda q, oc=oc, sg=sg, qp=qp: q.scalar_tensor_tensor(
                    qstage[oc % 2][:, sg * 512:(sg + 1) * 512],
                    fm[qp][:], -1.0, fp[qp][:], ALU.add, ALU.add),
                    reads=[t_fm[qp], t_fp[qp]], writes=[t_qst[oc % 2]])
            P.dma("scalar", lambda q, oc=oc: q.dma_start(
                out=q_scr[oc].ap(), in_=qstage[oc % 2][:]),
                reads=[t_qst[oc % 2]], sem=f"Sq{oc % 2}")
        transposes(OC - 1)
        P.flush(blk)


def _phase_A2(P, nc, blk, b, t, wv_in, brow_in, sb, psf):
    with ExitStack() as es:
        wvb = [sb(es, "wvb0", [128, II], BF16), sb(es, "wvb1", [128, II], BF16)]
        vt = [sb(es, f"vt{i}", [128, 512], BF16) for i in range(2)]
        fv = [sb(es, f"fv{i}", [128, 512], F32) for i in range(2)]
        ps_v = [psf(es) for _ in range(2)]
        ps_kv = [psf(es) for _ in range(2)]
        ps_rep = psf(es)
        t_wv = [Buf(), Buf()]
        t_vt = [Buf(), Buf()]
        t_fv = [Buf(), Buf()]
        t_psv = [Buf(), Buf()]
        t_pskv = [Buf(), Buf()]
        t_psrep = Buf()

        def kv_part_mms(og, scx):
            idx = og * SC + scx
            kp = idx % 2
            vp = idx % 2
            for hl in range(4):
                oc = og * 4 + hl
                P.emit("tensor", lambda q, oc=oc, scx=scx, hl=hl, kp=kp, vp=vp: q.matmul(
                    ps_kv[kp][:, hl * 128:(hl + 1) * 128],
                    b.p2[oc][:, scx * 128:(scx + 1) * 128],
                    vt[vp][:, hl * 128:(hl + 1) * 128],
                    start=True, stop=True),
                    reads=[t.p2[oc], t_vt[vp]], writes=[t_pskv[kp]],
                    sig=(hl == 3))
            P.emit("vector", lambda q, og=og, kp=kp: q.tensor_tensor(
                b.kv_sb[:, og * 512:(og + 1) * 512],
                b.kv_sb[:, og * 512:(og + 1) * 512],
                ps_kv[kp][:], ALU.add),
                reads=[t_pskv[kp], t.kv[og]], writes=[t.kv[og]])

        P.dma("sync", lambda q: q.dma_start(out=b.brow2[:], in_=brow_in[1:2, :]),
              writes=[t.brow2], sem="Lbo")
        for og in range(4):
            wp = og % 2
            P.dma("sync", lambda q, og=og: q.dma_start(
                out=b.brow[:], in_=brow_in[0:1, og * 512:(og + 1) * 512]),
                writes=[t.brow], sem="Lbr")
            P.emit("tensor", lambda q: q.matmul(
                ps_rep[:], b.ones_row[:], b.brow[:], start=True, stop=True),
                reads=[t.brow, t.setup], writes=[t_psrep])
            P.emit("vector", lambda q, og=og: q.tensor_copy(
                b.brep[:, og * 512:(og + 1) * 512], ps_rep[:]),
                reads=[t_psrep], writes=[t.brep[og]])
            P.dma("gpsimd", lambda q, og=og, wp=wp: q.dma_start(
                out=wvb[wp][:], in_=wv_in[og]),
                writes=[t_wv[wp]], sem=f"Lwv{wp}")
            for scx in range(SC):
                vp = (og * SC + scx) % 2
                for fcx in range(FC):
                    P.emit("tensor", lambda q, wp=wp, fcx=fcx, scx=scx, vp=vp: q.matmul(
                        ps_v[vp][:], b.xn[fcx][:, scx * 128:(scx + 1) * 128],
                        wvb[wp][:, fcx * 512:(fcx + 1) * 512],
                        start=(fcx == 0), stop=(fcx == FC - 1)),
                        reads=[t_wv[wp], t.xn[fcx][scx // 4]], writes=[t_psv[vp]],
                        sig=(fcx == FC - 1))
                P.emit("vector", lambda q, og=og, vp=vp: q.tensor_tensor(
                    fv[vp][:], ps_v[vp][:], b.brep[:, og * 512:(og + 1) * 512], ALU.add),
                    reads=[t_psv[vp], t.brep[og]], writes=[t_fv[vp]])
                P.emit("scalar", lambda q, vp=vp: q.activation(vt[vp][:], fv[vp][:], AF.Silu),
                       reads=[t_fv[vp]], writes=[t_vt[vp]])
                if scx >= 1:
                    kv_part_mms(og, scx - 1)
            kv_part_mms(og, SC - 1)
        P.flush(blk)


def _phase_AB(P, nc, blk, b, t, wu_in, wo_in, u_scr, kv_part, kv_sum, brow_in, dbg, sb, psf):
    P.dma("gpsimd", lambda q: q.dma_start(out=kv_part.ap(), in_=b.kv_sb[:]),
          reads=[t.kv[0], t.kv[1], t.kv[2], t.kv[3]], sem="Skv")
    ev_cc = P.emit("gpsimd", lambda q: q.collective_compute(
        "AllReduce", ALU.add, replica_groups=[list(range(NCORE))],
        ins=[kv_part.ap().opt()], outs=[kv_sum.ap().opt()]),
        waits=[("Skv", 16)], sig=True, sem="cc", by=1, known=False)
    with ExitStack() as es:
        wub = [sb(es, "wub0", [128, D], BF16), sb(es, "wub1", [128, D], BF16)]
        fu = [sb(es, f"fuU{i}", [128, 512], F32) for i in range(2)]
        ps_u = [psf(es) for _ in range(2)]
        ps_rep = [psf(es) for _ in range(2)]
        t_rep = [Buf(), Buf()]
        for og in range(4):
            rp = og % 2
            P.emit("tensor", lambda q, og=og, rp=rp: q.matmul(
                ps_rep[rp][:], b.ones_row[:], b.brow2[0:1, og * 512:(og + 1) * 512],
                start=True, stop=True),
                reads=[t.brow2, t.setup], writes=[t_rep[rp]])
            P.emit("vector", lambda q, og=og, rp=rp: q.tensor_copy(
                b.brep[:, og * 512:(og + 1) * 512], ps_rep[rp][:]),
                reads=[t_rep[rp]], writes=[t.brep[og]])
        t_wu = [Buf(), Buf()]
        t_fu = [Buf(), Buf()]
        t_psu = [Buf(), Buf()]

        for oc in range(OC):
            wp = oc % 2
            P.dma("gpsimd", lambda q, oc=oc, wp=wp: q.dma_start(
                out=wub[wp][:], in_=wu_in[oc]),
                writes=[t_wu[wp]], sem=f"Lwu{wp}")
            P.dma("gpsimd", lambda q, oc=oc: q.dma_start(
                out=b.p2[oc][:], in_=wo_in[oc * 128:(oc + 1) * 128, :]),
                writes=[t.p2[oc]], sem="Lwo")
            for sg in range(SG):
                up = (oc * SG + sg) % 2
                for fcx in range(FC):
                    P.emit("tensor", lambda q, wp=wp, fcx=fcx, sg=sg, up=up: q.matmul(
                        ps_u[up][:], wub[wp][:, fcx * 128:(fcx + 1) * 128],
                        b.xn[fcx][:, sg * 512:(sg + 1) * 512],
                        start=(fcx == 0), stop=(fcx == FC - 1)),
                        reads=[t_wu[wp], t.xn[fcx][sg]], writes=[t_psu[up]],
                        sig=(fcx == FC - 1))
                P.emit("scalar", lambda q, oc=oc, up=up: q.activation(
                    fu[up][:], ps_u[up][:], AF.Silu,
                    bias=b.pp[:, PP_BU + oc:PP_BU + oc + 1]),
                    reads=[t_psu[up], t.pp], writes=[t_fu[up]])
                P.dma("scalar", lambda q, oc=oc, sg=sg, up=up: q.dma_start(
                    out=u_scr[oc].ap()[:, sg * 512:(sg + 1) * 512], in_=fu[up][:]),
                    reads=[t_fu[up]], sem=f"Su{up}")
        P.dma("gpsimd", lambda q: q.dma_start(out=b.kv_bf[:], in_=kv_sum.ap()),
              writes=[t.kvbf], waits=[ev_cc], sem="Lkv")
        if dbg:
            P.dma("gpsimd", lambda q: q.dma_start(out=dbg["kv"], in_=kv_sum.ap()),
                  sem="Sdbg")
        for ec in range(16):
            t.p2[ec].w = ("Lwo", 16 * 16)
            t.p2[ec].readers = []
        P.flush(blk)


def _phase_B(P, nc, blk, b, t, q_scr, u_scr, af_scr, brow_in, sb, psf):
    with ExitStack() as es:
        attn = [sb(es, f"attn{i}", [128, 512], BF16) for i in range(2)]
        qs = [sb(es, f"qsB{i}", [128, 512], BF16) for i in range(16)]
        b.rows = sb(es, "rowsB", [1, 2048], F32)
        ybuf = sb(es, "ybuf", [128, II], BF16)
        sq = [sb(es, f"sqB{i}", [128, 512], BF16) for i in range(2)]
        yt = [sb(es, f"ytB{i}", [128, 512], F32) for i in range(2)]
        fub = [sb(es, f"fubB{i}", [128, 512], F32) for i in range(2)]
        fab = [sb(es, f"fabB{i}", [128, 512], F32) for i in range(2)]
        rrep = sb(es, "rrepB", [128, 512], F32)
        nrep = sb(es, "nrepB", [128, 512], F32)
        ps_e2 = [psf(es) for _ in range(2)]
        ps_s1 = psf(es, (1, 512), F32)
        ps_s2 = psf(es, (1, 512), F32)
        ps_rp = [psf(es) for _ in range(2)]
        ps_wo = [psf(es) for _ in range(2)]

        t_attn = [Buf(), Buf()]
        t_qs = [Buf() for _ in range(16)]
        t_y = [Buf() for _ in range(16)]
        t_sq = [Buf(), Buf()]
        t_yt = [Buf(), Buf()]
        t_fub = [Buf(), Buf()]
        t_fab = [Buf(), Buf()]
        t_rrep = Buf()
        t_nrep = Buf()
        t_pse2 = [Buf(), Buf()]
        t_s1 = Buf()
        t_s2 = Buf()
        t_rp = [Buf(), Buf()]
        t_wo = [Buf(), Buf()]
        t_row = [Buf() for _ in range(4)]

        R = lambda j: b.rows[0:1, j * 512:(j + 1) * 512]

        def wo_proj(sg, jr=range(16)):
            for j in jr:
                    sc_loc, og = j // 4, j % 4
                    wp = j % 2
                    for ec in range(16):
                        P.emit("tensor", lambda q, ec=ec, sc_loc=sc_loc, og=og, wp=wp: q.matmul(
                            ps_wo[wp][:],
                            ybuf[:, ec * 512 + sc_loc * 128: ec * 512 + sc_loc * 128 + 128],
                            b.p2[ec][:, og * 512:(og + 1) * 512],
                            start=(ec == 0), stop=(ec == 15)),
                            reads=[t_y[ec], t.p2[ec]], writes=[t_wo[wp]],
                            sig=(ec == 15))
                    P.emit("vector", lambda q, og=og, wp=wp, j=j: q.tensor_tensor(
                        fab[j % 2][:], ps_wo[wp][:], b.brep[:, og * 512:(og + 1) * 512], ALU.add),
                        reads=[t_wo[wp], t.brep[og]], writes=[t_fab[j % 2]])
                    P.dma("sync", lambda q, sg=sg, sc_loc=sc_loc, og=og, j=j: q.dma_start(
                        out=af_scr.ap()[sg * 512 + sc_loc * 128: sg * 512 + (sc_loc + 1) * 128,
                                        og * 512:(og + 1) * 512],
                        in_=fab[j % 2][:]),
                        reads=[t_fab[j % 2]], sem=f"Saf{j % 2}")

        def e2_mm(sg, h, first):
            ep = h % 2
            if first:
                P.dma("scalar", lambda q, h=h, sg=sg: q.dma_start(
                    out=qs[h][:], in_=q_scr[h].ap()[:, sg * 512:(sg + 1) * 512]),
                    writes=[t_qs[h]], sem=f"Lq{h}")
            P.emit("tensor", lambda q, h=h, ep=ep: q.matmul(
                ps_e2[ep][:], b.kv_bf[:, h * 128:(h + 1) * 128], qs[h][:],
                start=True, stop=True),
                reads=[t.kvbf, t_qs[h]], writes=[t_pse2[ep]])

        def stats_h(sg, h):
            ep = h % 2
            P.emit("vector", lambda q, ep=ep: q.tensor_copy(attn[ep][:], ps_e2[ep][:]),
                   reads=[t_pse2[ep]], writes=[t_attn[ep]])
            P.emit("vector", lambda q, ep=ep: q.tensor_tensor(sq[ep][:], attn[ep][:], attn[ep][:], ALU.mult),
                   reads=[t_attn[ep]], writes=[t_sq[ep]])
            P.emit("tensor", lambda q, h=h, ep=ep: q.matmul(
                ps_s1[:], b.ones_col_bf[:], attn[ep][:], start=(h == 0), stop=(h == 15)),
                reads=[t_attn[ep], t.setup], writes=[t_s1], sig=(h == 15))
            P.emit("tensor", lambda q, h=h, ep=ep: q.matmul(
                ps_s2[:], b.ones_col_bf[:], sq[ep][:], start=(h == 0), stop=(h == 15)),
                reads=[t_sq[ep]], writes=[t_s2], sig=(h == 15))

        def y_h(sg, h):
            ep = h % 2
            P.emit("vector", lambda q, ep=ep: q.tensor_tensor(
                yt[ep][:], ps_e2[ep][:], rrep[:], ALU.mult),
                reads=[t_pse2[ep], t_rrep], writes=[t_yt[ep]])
            P.emit("vector", lambda q, ep=ep: q.tensor_tensor(
                yt[ep][:], yt[ep][:], nrep[:], ALU.add),
                reads=[t_nrep], writes=[t_yt[ep]])
            P.emit("vector", lambda q, h=h, ep=ep: q.tensor_scalar(
                yt[ep][:], yt[ep][:],
                b.pp[:, PP_LNG + h:PP_LNG + h + 1], b.pp[:, PP_LNB + h:PP_LNB + h + 1],
                ALU.mult, ALU.add),
                reads=[t.pp], writes=[t_yt[ep]])
            P.dma("scalar", lambda q, h=h, sg=sg, ep=ep: q.dma_start(
                out=fub[ep][:], in_=u_scr[h].ap()[:, sg * 512:(sg + 1) * 512]),
                writes=[t_fub[ep]], sem=f"LuB{ep}")
            P.emit("vector", lambda q, h=h, ep=ep: q.tensor_tensor(
                ybuf[:, h * 512:(h + 1) * 512], yt[ep][:], fub[ep][:], ALU.mult),
                reads=[t_yt[ep], t_fub[ep]], writes=[t_y[h]])

        for sg in range(SG):
            for h in range(16):
                e2_mm(sg, h, True)
                if h >= 1:
                    stats_h(sg, h - 1)
            stats_h(sg, 15)
            if sg >= 1:
                wo_proj(sg - 1, range(8, 16))
            P.emit("scalar", lambda q: q.activation(R(0), ps_s1[:], AF.Copy, scale=1.0 / D),
                   reads=[t_s1], writes=[t_row[0]])
            P.emit("scalar", lambda q: q.activation(R(1), ps_s2[:], AF.Copy, scale=1.0 / D),
                   reads=[t_s2], writes=[t_row[1]])
            P.emit("scalar", lambda q: q.activation(R(2), R(0), AF.Square),
                   reads=[t_row[0]], writes=[t_row[2]])
            P.emit("vector", lambda q: q.tensor_tensor(R(3), R(1), R(2), ALU.subtract),
                   reads=[t_row[1], t_row[2]], writes=[t_row[3]])
            P.emit("scalar", lambda q: q.activation(R(2), R(3), AF.Ln, bias=b.eps_row[:]),
                   reads=[t_row[3], t.setup], writes=[t_row[2]])
            P.emit("scalar", lambda q: q.activation(R(1), R(2), AF.Exp, scale=-0.5),
                   reads=[t_row[2]], writes=[t_row[1]])
            P.emit("vector", lambda q: q.tensor_tensor(R(2), R(0), R(1), ALU.mult),
                   reads=[t_row[0], t_row[1]], writes=[t_row[2]])
            P.emit("vector", lambda q: q.tensor_scalar(R(3), R(2), -1.0, None, ALU.mult),
                   reads=[t_row[2]], writes=[t_row[3]])
            P.emit("tensor", lambda q: q.matmul(ps_rp[0][:], b.ones_row[:], R(1), start=True, stop=True),
                   reads=[t_row[1], t.setup], writes=[t_rp[0]])
            P.emit("tensor", lambda q: q.matmul(ps_rp[1][:], b.ones_row[:], R(3), start=True, stop=True),
                   reads=[t_row[3]], writes=[t_rp[1]])
            P.emit("scalar", lambda q: q.copy(rrep[:], ps_rp[0][:]), reads=[t_rp[0]], writes=[t_rrep])
            P.emit("scalar", lambda q: q.copy(nrep[:], ps_rp[1][:]), reads=[t_rp[1]], writes=[t_nrep])
            for h in range(16):
                e2_mm(sg, h, False)
                if h >= 1:
                    y_h(sg, h - 1)
            y_h(sg, 15)
            wo_proj(sg, range(0, 8))
        wo_proj(SG - 1, range(8, 16))
        P.flush(blk)


def _phase_F(P, nc, blk, b, t, x_in, guw_in, wd_in, af_scr, out_t, sb, psf):
    with ExitStack() as es:
        guw = [sb(es, f"guwF{i}", [128, 4096], BF16) for i in range(2)]
        wdb = [sb(es, f"wdF{i}", [128, II // 2], BF16) for i in range(3)]
        ftm = [sb(es, f"ftm{j}", [128, 512], F32) for j in range(4)]
        fgc = [sb(es, f"fgc{i}", [128, 512], F32) for i in range(2)]
        fuc = [sb(es, f"fuc{i}", [128, 512], F32) for i in range(2)]
        fds = [sb(es, f"fds{i}", [128, 512], F32) for i in range(2)]
        fxc = [sb(es, f"fxc{i}", [128, 512], F32) for i in range(2)]
        fac = [sb(es, f"fac{i}", [128, 512], F32) for i in range(2)]
        rs_rep = [sb(es, f"rs_rep{i}", [128, 512], F32) for i in range(2)]
        cs_rep = sb(es, "cs_rep", [128, 512], F32)
        rowf = sb(es, "rowf", [1, 512], F32)
        ident_f = sb(es, "ident_fF", [128, 128], F32)
        t_identf = Buf()
        P.emit("gpsimd", lambda q: q.memset(ident_f[:], 0.0), writes=[t_identf])
        P.emit("gpsimd", lambda q: q.affine_select(
            out=ident_f[:], in_=ident_f[:], compare_op=ALU.not_equal,
            fill=1.0, base=0, pattern=[[-1, 128]], channel_multiplier=1), writes=[t_identf])
        ps_g = [psf(es) for _ in range(2)]
        ps_u = [psf(es) for _ in range(2)]
        ps_d = [psf(es) for _ in range(2)]
        ps_sc = psf(es, (1, 512), F32)
        ps_t = psf(es)

        t_guw = [Buf(), Buf()]
        t_wd = [Buf(), Buf(), Buf()]
        t_ftm = [Buf() for _ in range(4)]
        t_fgc = [Buf(), Buf()]
        t_fuc = [Buf(), Buf()]
        t_fds = [Buf(), Buf()]
        t_fxc = [Buf(), Buf()]
        t_fac = [Buf(), Buf()]
        t_rsrep = [Buf(), Buf()]
        t_csrep = Buf()
        t_psg = [Buf(), Buf()]
        t_psu = [Buf(), Buf()]
        t_psd = [Buf(), Buf()]
        t_pssc = Buf()
        t_pst = Buf()
        t_act = [Buf() for _ in range(ICN)]
        t_rowF = Buf()

        R = lambda j: rowf[:]
        act_ap = lambda ic: b.p2[ic // 4][:, (ic % 4) * 512:(ic % 4) * 512 + 512]

        def evac_hc(sg, hc):
            dp = hc % 2
            hg, hl = hc // 4, hc % 4
            P.emit("vector", lambda q, dp=dp: q.tensor_tensor(
                fds[dp][:], ps_d[dp][:], cs_rep[:], ALU.subtract),
                reads=[t_psd[dp], t_csrep], writes=[t_fds[dp]])
            P.emit("scalar", lambda q, hc=hc, dp=dp: q.activation(
                fds[dp][:], fds[dp][:], AF.Copy,
                scale=b.pp[:, PP_DS + hc:PP_DS + hc + 1]),
                reads=[t.pp], writes=[t_fds[dp]])
            for j in range(4):
                P.emit("tensor", lambda q, dp=dp, j=j: q.transpose(
                    ps_t[:, j * 128:(j + 1) * 128],
                    fds[dp][:, j * 128:(j + 1) * 128], ident_f[:]),
                    reads=[t_fds[dp], t_identf], writes=[t_pst], sig=(j == 3))
            for j in range(4):
                P.emit("scalar", lambda q, j=j, hl=hl: q.copy(
                    ftm[j][:, hl * 128:(hl + 1) * 128],
                    ps_t[:, j * 128:(j + 1) * 128]),
                    reads=[t_pst], writes=[t_ftm[j]], sig=(j == 3))

        def adds_hg(sg, hg):
            for j in range(4):
                xp = j % 2
                rowa = sg * 512 + j * 128
                P.dma("sync", lambda q, rowa=rowa, hg=hg, xp=xp: q.dma_start(
                    out=fxc[xp][:], in_=x_in[rowa:rowa + 128, hg * 512:(hg + 1) * 512]),
                    writes=[t_fxc[xp]], sem=f"Lxf{xp}")
                P.dma("sync", lambda q, rowa=rowa, hg=hg, xp=xp: q.dma_start(
                    out=fac[xp][:], in_=af_scr.ap()[rowa:rowa + 128, hg * 512:(hg + 1) * 512]),
                    writes=[t_fac[xp]], sem=f"Laf{xp}")
                P.emit("vector", lambda q, j=j, xp=xp: q.tensor_tensor(
                    ftm[j][:], ftm[j][:], fxc[xp][:], ALU.add),
                    reads=[t_fxc[xp]], writes=[t_ftm[j]])
                P.emit("vector", lambda q, j=j, xp=xp: q.tensor_tensor(
                    ftm[j][:], ftm[j][:], fac[xp][:], ALU.add),
                    reads=[t_fac[xp]], writes=[t_ftm[j]])
                P.dma("scalar", lambda q, j=j, hg=hg, rowa=rowa: q.dma_start(
                    out=out_t[rowa:rowa + 128, hg * 512:(hg + 1) * 512],
                    in_=ftm[j][:]),
                    reads=[t_ftm[j]], sem=f"So{j}")

        def rs_chain(sg2):
            rp = sg2 % 2
            cols2 = slice(sg2 * 512, (sg2 + 1) * 512)
            for fcx in range(FC):
                P.emit("tensor", lambda q, fcx=fcx, cols2=cols2: q.matmul(
                    ps_sc[:], b.ones_col_bf[:], b.xn[fcx][:, cols2],
                    start=(fcx == 0), stop=(fcx == FC - 1)),
                    reads=[t.xn[fcx][sg2], t.setup], writes=[t_pssc], sig=(fcx == FC - 1))
            P.emit("scalar", lambda q: q.activation(R(0), ps_sc[:], AF.Copy, scale=128.0),
                   reads=[t_pssc], writes=[t_rowF])
            P.emit("tensor", lambda q: q.matmul(ps_t[:], b.ones_row[:], R(0), start=True, stop=True),
                   reads=[t_rowF, t.setup], writes=[t_pst])
            P.emit("scalar", lambda q, rp=rp: q.copy(rs_rep[rp][:], ps_t[:]),
                   reads=[t_pst], writes=[t_rsrep[rp]])

        for sg in range(SG):
            cols = slice(sg * 512, (sg + 1) * 512)
            if sg == 0:
                rs_chain(0)
            for ic in range(ICN):
                gp = ic % 2
                P.dma("gpsimd", lambda q, ic=ic, gp=gp: q.dma_start(
                    out=guw[gp][:], in_=guw_in[ic]),
                    writes=[t_guw[gp]], sem=f"Lgu{gp}")
                for fcx in range(FC):
                    P.emit("tensor", lambda q, gp=gp, fcx=fcx, cols=cols: q.matmul(
                        ps_g[gp][:], guw[gp][:, fcx * 256:fcx * 256 + 128],
                        b.xn[fcx][:, cols], start=(fcx == 0), stop=(fcx == FC - 1)),
                        reads=[t_guw[gp], t.xn[fcx][sg]], writes=[t_psg[gp]], sig=(fcx == FC - 1))
                for fcx in range(FC):
                    P.emit("tensor", lambda q, gp=gp, fcx=fcx, cols=cols: q.matmul(
                        ps_u[gp][:], guw[gp][:, fcx * 256 + 128:fcx * 256 + 256],
                        b.xn[fcx][:, cols], start=(fcx == 0), stop=(fcx == FC - 1)),
                        reads=[t_guw[gp], t.xn[fcx][sg]], writes=[t_psu[gp]], sig=(fcx == FC - 1))
                P.emit("vector", lambda q, gp=gp, sg=sg: q.tensor_tensor(
                    fgc[gp][:], ps_g[gp][:], rs_rep[sg % 2][:], ALU.subtract),
                    reads=[t_psg[gp], t_rsrep[sg % 2]], writes=[t_fgc[gp]])
                P.emit("scalar", lambda q, ic=ic, gp=gp: q.activation(
                    fgc[gp][:], fgc[gp][:], AF.Silu,
                    scale=b.pp[:, PP_GS + ic:PP_GS + ic + 1]),
                    reads=[t.pp], writes=[t_fgc[gp]])
                P.emit("vector", lambda q, gp=gp, sg=sg: q.tensor_tensor(
                    fuc[gp][:], ps_u[gp][:], rs_rep[sg % 2][:], ALU.subtract),
                    reads=[t_psu[gp], t_rsrep[sg % 2]], writes=[t_fuc[gp]])
                P.emit("vector", lambda q, ic=ic, gp=gp: q.scalar_tensor_tensor(
                    act_ap(ic), fuc[gp][:], b.pp[:, PP_US + ic:PP_US + ic + 1], fgc[gp][:],
                    ALU.mult, ALU.mult),
                    reads=[t_fuc[gp], t_fgc[gp], t.pp], writes=[t_act[ic]])
            for ic in range(ICN):
                P.emit("tensor", lambda q, ic=ic: q.matmul(
                    ps_sc[:], b.ones_col_bf[:], act_ap(ic),
                    start=(ic == 0), stop=(ic == ICN - 1)),
                    reads=[t_act[ic], t.setup], writes=[t_pssc], sig=(ic == ICN - 1))
            P.emit("scalar", lambda q: q.activation(R(0), ps_sc[:], AF.Copy, scale=128.0),
                   reads=[t_pssc], writes=[t_rowF])
            P.emit("tensor", lambda q: q.matmul(ps_t[:], b.ones_row[:], R(0), start=True, stop=True),
                   reads=[t_rowF], writes=[t_pst])
            P.emit("scalar", lambda q: q.copy(cs_rep[:], ps_t[:]), reads=[t_pst], writes=[t_csrep])
            if sg + 1 < SG:
                rs_chain(sg + 1)
            for hc in range(HCN):
                dp = hc % 2
                for k in range(2):
                    hp = (hc * 2 + k) % 3
                    P.dma("gpsimd", lambda q, hc=hc, k=k, hp=hp: q.dma_start(
                        out=wdb[hp][:], in_=wd_in[hc, :, k * 4096:(k + 1) * 4096]),
                        writes=[t_wd[hp]], sem=f"Lwd{hp}")
                for ic in range(ICN):
                    hp = (hc * 2 + ic // 32) % 3
                    P.emit("tensor", lambda q, dp=dp, hp=hp, ic=ic: q.matmul(
                        ps_d[dp][:],
                        wdb[hp][:, (ic % 32) * 128:(ic % 32 + 1) * 128], act_ap(ic),
                        start=(ic == 0), stop=(ic == ICN - 1)),
                        reads=[t_wd[hp], t_act[ic]], writes=[t_psd[dp]],
                        sig=(ic == ICN - 1))
                if hc >= 1:
                    evac_hc(sg, hc - 1)
                    if hc % 4 == 0:
                        adds_hg(sg, hc // 4 - 1)
            evac_hc(sg, HCN - 1)
            adds_hg(sg, 3)
        fin = [(f"So{j}", P.cnt.get(f"So{j}", 0)) for j in range(4)]
        if "Sdbg" in P.cnt:
            fin.append(("Sdbg", P.cnt["Sdbg"]))
        P.emit("sync", lambda q: q.nop(), waits=fin, sig=False)
        P.flush(blk)



def _repack_u8(a_int32):
    a = np.ascontiguousarray(np.asarray(a_int32))
    assert a.dtype == np.int32
    return np.ascontiguousarray(a.view(np.uint8).reshape(*a.shape, 4)[..., 0])


def _prep_inputs(inp):
    f = lambda v: np.ascontiguousarray(np.asarray(v, dtype=np.float32))
    x = f(inp["L0_attn_x"]).reshape(NCORE, S, D)
    wqk = np.ascontiguousarray(
        f(inp["L0_attn_Wqk"]).reshape(16, 128, OC, 128).transpose(2, 1, 0, 3).reshape(OC, 128, D))
    wv = np.ascontiguousarray(
        f(inp["L0_attn_Wv"]).reshape(16, 128, 4, 512).transpose(2, 1, 0, 3).reshape(4, 128, II))
    wu = np.ascontiguousarray(
        f(inp["L0_attn_Wu"]).reshape(16, 128, OC, 128).transpose(2, 1, 0, 3).reshape(OC, 128, D))
    wo = f(inp["L0_attn_Wo"])
    pp = np.zeros((128, PP_COLS), np.float32)
    pp[:, PP_BQK:PP_BQK + 16] = f(inp["L0_attn_bqk"]).reshape(16, 128).T
    pp[:, PP_BU:PP_BU + 16] = f(inp["L0_attn_bu"]).reshape(16, 128).T
    pp[:, PP_LNG:PP_LNG + 16] = f(inp["L0_attn_ln_g"]).reshape(16, 128).T
    pp[:, PP_LNB:PP_LNB + 16] = f(inp["L0_attn_ln_b"]).reshape(16, 128).T
    pp[:, PP_GS:PP_GS + 64] = f(inp["L0_ffn_gate_scale"]).reshape(64, 128).T
    pp[:, PP_US:PP_US + 64] = f(inp["L0_ffn_up_scale"]).reshape(64, 128).T
    pp[:, PP_DS:PP_DS + 16] = f(inp["L0_ffn_down_scale"]).reshape(16, 128).T
    brow = np.ascontiguousarray(np.stack([f(inp["L0_attn_bv"]), f(inp["L0_attn_bo"])]))
    g8 = _repack_u8(inp["L0_ffn_gate_w"])
    u8 = _repack_u8(inp["L0_ffn_up_w"])
    gt = g8.T.reshape(16, 128, ICN, 128)
    ut = u8.T.reshape(16, 128, ICN, 128)
    guw = np.ascontiguousarray(
        np.concatenate([gt[:, :, :, None, :], ut[:, :, :, None, :]], axis=3)
        .transpose(2, 1, 0, 3, 4).reshape(ICN, 128, 4096))
    d8 = _repack_u8(inp["L0_ffn_down_w"])
    wd = np.ascontiguousarray(
        d8.T.reshape(ICN, 128, HCN, 128).transpose(2, 1, 0, 3).reshape(HCN, 128, II))
    const = {"wqk": wqk, "wv": wv, "wu": wu, "wo": wo, "pp": pp, "brow": brow,
             "guw": guw, "wd": wd}
    return x, const


def kernel(**inputs):
    x, const = _prep_inputs(inputs)
    nc = build(stage=3, debug=False)
    in_maps = [dict(const, x=np.ascontiguousarray(x[c])) for c in range(NCORE)]
    res = run_bass_kernel_spmd(nc, in_maps, core_ids=list(range(NCORE)))
    out = np.concatenate([np.asarray(res.results[c]["out"]) for c in range(NCORE)], axis=0)
    return out.reshape(1, NCORE * S, D).astype(np.float32)


if __name__ == "__main__":
    import time
    t0 = time.time()
    nc = build(stage=3, debug=False)
    print(f"built in {time.time() - t0:.1f}s")
